# version 52
# baseline (speedup 1.0000x reference)
import numpy as np
import ml_dtypes
import concourse.bass as bass
import concourse.mybir as mybir
from concourse.bass_utils import run_bass_kernel_spmd

F32 = mybir.dt.float32
BF16 = mybir.dt.bfloat16
AF = mybir.ActivationFunctionType
ALU = mybir.AluOpType

B, T, C = 16, 2048, 1024
NCORE = 8
H, D = 16, 64
TILE = 512
NTILE = 8
NG = 4
EPS_LN = 1e-5
EPS_AT = 1e-6
INV_C = 1.0 / 1024.0

ENGS = ("sync", "tensor", "vector", "scalar", "gpsimd")


class Res:
    __slots__ = ("w", "r")

    def __init__(self):
        self.w = []
        self.r = []


class Sched:

    def __init__(self):
        self.ops = []
        self.forced = set()
        self.chained = set()

    def emit(self, eng, fn, reads=(), writes=(), deps=(), dma=False, force_inc=False, chain=False):
        deps = set(deps)
        for r in reads:
            deps.update(r.w)
        for r in writes:
            deps.update(r.w)
            deps.update(r.r)
        i = len(self.ops)
        self.ops.append((eng, fn, sorted(deps), 16 if dma else 1))
        if chain:
            self.chained.add(i)
        if force_inc or dma:
            self.forced.add(i)
        for r in reads:
            r.r.append(i)
        for r in writes:
            r.w = [i]
            r.r = []
        return i

    def replay(self, sems):
        n = len(self.ops)
        need = [False] * n
        for i in self.forced:
            need[i] = True
        self_sync = {}
        prev_own = {}
        for i, (eng, fn, deps, amt) in enumerate(self.ops):
            p = prev_own.get(eng)
            if p is not None and p in deps and i not in self.chained:
                self_sync[i] = [p]
                need[p] = True
            prev_own[eng] = i
        for eng, fn, deps, amt in self.ops:
            for d in deps:
                if self.ops[d][0] != eng:
                    need[d] = True
        val = [0] * n
        cnt = {e: 0 for e in ENGS}
        for i, (eng, fn, deps, amt) in enumerate(self.ops):
            if need[i]:
                cnt[eng] += amt
                val[i] = cnt[eng]
        self.final_cnt = cnt

        def run_engine(engine_obj, ename):
            waited = {e: 0 for e in ENGS}
            own_dma = 0
            for i, (eng, fn, deps, amt) in enumerate(self.ops):
                if eng != ename:
                    continue
                if amt == 16:
                    if own_dma > 0 and waited[ename] < 16 * own_dma:
                        engine_obj.wait_ge(sems[ename], 16 * own_dma)
                        waited[ename] = 16 * own_dma
                    own_dma += 1
                want = {}
                for d in deps:
                    de = self.ops[d][0]
                    if de == ename:
                        continue
                    v = val[d]
                    assert v > 0, f"dep {d} of op {i} has no inc"
                    if v > want.get(de, 0):
                        want[de] = v
                for d in self_sync.get(i, ()):
                    v = val[d]
                    if v > want.get(ename, 0):
                        want[ename] = v
                for de, v in want.items():
                    if waited[de] < v:
                        engine_obj.wait_ge(sems[de], v)
                        waited[de] = v
                ins = fn(engine_obj)
                if need[i]:
                    ins.then_inc(sems[eng], amt)
        return run_engine


def build_program():
    nc = bass.Bass("TRN2", target_bir_lowering=False)

    x_d = nc.dram_tensor("x", [2, T, C], F32, kind="ExternalInput").ap()
    out_d = nc.dram_tensor("out", [2, T, C], F32, kind="ExternalOutput").ap()
    wa_d = nc.dram_tensor("w_attn", [C, 3 * C], F32, kind="ExternalInput").ap()
    wp_d = nc.dram_tensor("w_proj", [C, C], F32, kind="ExternalInput").ap()
    win_d = nc.dram_tensor("win_r", [128, 1024], F32, kind="ExternalInput").ap()
    bd1_d = nc.dram_tensor("bd1", [64, 1024], F32, kind="ExternalInput").ap()
    bd2_d = nc.dram_tensor("bd2_r", [128, 1024], F32, kind="ExternalInput").ap()
    wout_d = nc.dram_tensor("wout", [64, 1024], F32, kind="ExternalInput").ap()
    bout_d = nc.dram_tensor("bout_r", [1, 1024], F32, kind="ExternalInput").ap()
    binb_d = nc.dram_tensor("binb", [64, 1], F32, kind="ExternalInput").ap()
    b1f_d = nc.dram_tensor("b1f", [128, 8], F32, kind="ExternalInput").ap()
    b2b_d = nc.dram_tensor("b2b", [64, 1], F32, kind="ExternalInput").ap()
    id_d = nc.dram_tensor("ident", [128, 128], BF16, kind="ExternalInput").ap()
    mask_d = nc.dram_tensor("mask4", [128, 128], BF16, kind="ExternalInput").ap()

    from contextlib import ExitStack
    with ExitStack() as _ctx:
        def sb(name, shape, dt):
            return _ctx.enter_context(nc.sbuf_tensor(name, shape, dt))
        wa_sb = sb("wa_sb", [128, 8 * 3072], BF16)
        wp_sb = sb("wp_sb", [128, 8 * 1024], BF16)
        win_sb = sb("win_sb", [128, 1024], BF16)
        bd1_sb = sb("bd1_sb", [64, 1024], BF16)
        bd2_sb = sb("bd2_sb", [128, 1024], BF16)
        woa_sb = sb("woa_sb", [65, 1024], BF16)
        ident = sb("ident_sb", [128, 128], BF16)
        mask4 = sb("mask4_sb", [128, 128], BF16)
        binb = sb("binb_sb", [64, 1], F32)
        b1f = sb("b1f_sb", [128, 8], F32)
        b2b = sb("b2b_sb", [64, 1], F32)
        stgA = sb("stgA", [128, 1536], F32)
        stgB = sb("stgB", [128, 1536], F32)
        X = [sb("X0", [128, 4096], F32), sb("X1", [128, 4096], F32)]
        xh = [sb("xh0", [128, 4096], BF16), sb("xh1", [128, 4096], BF16)]
        xf = [sb("xf0", [128, 4096], BF16), sb("xf1", [128, 4096], BF16)]
        qsb = [sb("qsb0", [128, 4096], BF16), sb("qsb1", [128, 4096], BF16)]
        ksb = [sb("ksb0", [128, 4096], BF16), sb("ksb1", [128, 4096], BF16)]
        vsb = [sb("vsb0", [128, 4224], BF16), sb("vsb1", [128, 4224], BF16)]
        ssbf = [sb("ssbf0", [64, 512], BF16), sb("ssbf1", [64, 512], BF16)]
        gsb = [sb("gsb0", [65, 512], BF16), sb("gsb1", [65, 512], BF16)]
        stats2 = [sb("stats0", [128, 32], F32), sb("stats1", [128, 32], F32)]
        accS = [sb("accS0", [128, 4], F32), sb("accS1", [128, 4], F32)]
        accVq = [sb("accVq0", [128, 4], F32), sb("accVq1", [128, 4], F32)]
        accVs = [sb("accVs0", [128, 4], F32), sb("accVs1", [128, 4], F32)]
        xsums = sb("xsums", [128, 16], F32)
        epsb = sb("epsb", [128, 1], F32)
        atsb = sb("atsb", [128, 2048], BF16)
        Ssb = sb("Ssb", [128, 520], BF16)
        dgat = sb("dgat", [128, 16], F32)
        rcp = sb("rcp", [128, 16], F32)
        stgA_bf = stgA[:, :].bitcast(BF16)
        stgB_bf = stgB[:, :].bitcast(BF16)
        etmp = stgA_bf[:, 0:512]
        rtmp = stgA_bf[:, 512:1024]
        yfc = stgA_bf[:, 1024:2048]
        ktmc = stgB_bf[:, 0:1024]
        zsb = stgB_bf[:, 1024:1284]
        ytm = stgB_bf[:, 1284:2308]
        pb = [_ctx.enter_context(nc.psum_tensor(f"pb{i}", [128, 512], F32)) for i in range(8)]
        sems = {e: _ctx.enter_context(nc.semaphore(f"sem_{e}")) for e in ENGS}
        block = _ctx.enter_context(nc.Block())

        S = Sched()
        R_pb = [Res() for _ in range(8)]
        R_stgA, R_stgB = Res(), Res()
        R_wa, R_wp, R_win, R_bd1, R_bd2, R_woa = (Res() for _ in range(6))
        R_id, R_mask, R_binb, R_b1f, R_b2b, R_zsb = (Res() for _ in range(6))
        R_X = [[Res() for _ in range(NG)] for _ in range(2)]
        R_xh = [[Res() for _ in range(NG)] for _ in range(2)]
        R_xf = [[Res() for _ in range(8)] for _ in range(2)]
        R_q = [[Res() for _ in range(8)] for _ in range(2)]
        R_k = [[Res() for _ in range(8)] for _ in range(2)]
        R_v = [[Res() for _ in range(NG)] for _ in range(2)]
        R_ssbf = [Res(), Res()]
        R_gsb = [Res(), Res()]
        R_ktmc = [Res(), Res()]
        R_at = [Res() for _ in range(4)]
        R_Ssb = [Res(), Res()]
        R_ytm = [Res(), Res()]
        R_yfc = [Res(), Res()]
        R_etmp, R_rtmp = Res(), Res()
        R_dumpv, R_dumps = Res(), Res()
        R_sums = [Res(), Res()]
        R_sumsq = [Res(), Res()]
        R_sumsqv = [Res(), Res()]
        R_mu = [Res(), Res()]
        R_musq = [Res(), Res()]
        R_varr = [Res(), Res()]
        R_stdv = [Res(), Res()]
        R_rr = [Res(), Res()]
        R_nmur = [Res(), Res()]
        R_dgat, R_rcp = Res(), Res()
        R_xsums = [Res(), Res()]
        R_epsb = Res()

        S.emit("sync", lambda e: e.dma_start(out=ident[:], in_=id_d), writes=[R_id], dma=True)
        S.emit("sync", lambda e: e.dma_start(out=mask4[:], in_=mask_d), writes=[R_mask], dma=True)
        S.emit("sync", lambda e: e.dma_start(out=binb[:], in_=binb_d), writes=[R_binb], dma=True)
        S.emit("sync", lambda e: e.dma_start(out=b1f[:], in_=b1f_d), writes=[R_b1f], dma=True)
        S.emit("sync", lambda e: e.dma_start(out=b2b[:], in_=b2b_d), writes=[R_b2b], dma=True)

        S.emit("vector", lambda e: e.memset(epsb[:, :], EPS_LN), writes=[R_epsb])
        for sl in range(2):
            S.emit("vector", lambda e, sl=sl: e.memset(gsb[sl][64:65, :], 1.0), writes=[R_gsb[sl]])
            for g in range(NG):
                ap = vsb[sl][:, 1056 * g:1056 * g + 1056].rearrange("p (h c) -> p h c", c=66)[:, :, 64]
                S.emit("vector", lambda e, ap=ap: e.memset(ap, 1.0), writes=[R_v[sl][g]])

        _stg_flip = [0]

        def staged(src_ap, rows, cols, dst_ap, wres, scale=None):
            stg, rstg = (stgA, R_stgA) if _stg_flip[0] == 0 else (stgB, R_stgB)
            q = "gpsimd" if _stg_flip[0] == 0 else "sync"
            _stg_flip[0] ^= 1
            sl = stg[0:rows, 0:cols]
            S.emit(q, lambda e, s=src_ap, d=sl: e.dma_start(out=d, in_=s),
                   writes=[rstg], dma=True)
            if scale is None:
                S.emit("vector", lambda e, d=sl, o=dst_ap: e.tensor_copy(out=o, in_=d),
                       reads=[rstg], writes=[wres])
            else:
                S.emit("vector", lambda e, d=sl, o=dst_ap: e.tensor_scalar(
                    out=o, in0=d, scalar1=scale, scalar2=None, op0=ALU.mult),
                    reads=[rstg], writes=[wres])

        staged(win_d, 128, 1024, win_sb[:], R_win)
        staged(bd1_d, 64, 1024, bd1_sb[:], R_bd1)
        staged(bd2_d, 128, 1024, bd2_sb[:], R_bd2)
        stg, rstg = (stgA, R_stgA) if _stg_flip[0] == 0 else (stgB, R_stgB)
        _stg_flip[0] ^= 1
        S.emit("gpsimd", lambda e, stg=stg: e.dma_start(out=stg[0:64, 0:1024], in_=wout_d),
               writes=[rstg], dma=True)
        S.emit("gpsimd", lambda e, stg=stg: e.dma_start(out=stg[64:65, 0:1024], in_=bout_d),
               writes=[rstg], dma=True)
        S.emit("vector", lambda e, stg=stg: e.tensor_scalar(
            out=woa_sb[0:65, :], in0=stg[0:65, 0:1024], scalar1=0.5, scalar2=None, op0=ALU.mult),
            reads=[rstg], writes=[R_woa])
        for j in range(8):
            for hlf in range(2):
                staged(wa_d[128 * j:128 * j + 128, 1536 * hlf:1536 * hlf + 1536], 128, 1536,
                       wa_sb[:, 3072 * j + 1536 * hlf:3072 * j + 1536 * hlf + 1536], R_wa)
        for j in range(8):
            staged(wp_d[128 * j:128 * j + 128, :], 128, 1024,
                   wp_sb[:, 1024 * j:1024 * j + 1024], R_wp)
        for r in (R_etmp, R_rtmp, R_yfc[0], R_yfc[1]):
            r.r.extend(R_stgA.r)
            r.w = list(R_stgA.w)
        for r in (R_ktmc[0], R_ktmc[1], R_zsb, R_ytm[0], R_ytm[1]):
            r.r.extend(R_stgB.r)
            r.w = list(R_stgB.w)
        S.emit("vector", lambda e: e.memset(zsb, 0.0), writes=[R_zsb])

        def ln_pass(sl, fb, folded=False):
            st = stats2[sl]
            sums = st[:, 0:4] if folded else accVs[sl][:, 0:4]
            sumsqS = accS[sl][:, 0:4]
            sumsqV = accVq[sl][:, 0:4]
            mu = st[:, 8:12]
            musq = st[:, 12:16]
            varr = st[:, 16:20]
            stdv = st[:, 20:24]
            rr = st[:, 24:28]
            nmur = st[:, 28:32]
            nv = 2 if folded else 1
            ns = NG - nv

            def emit_sq_v(g):
                S.emit("vector", lambda e, g=g, sl=sl, sumsqV=sumsqV: e.scalar_tensor_tensor(
                    out=xh[sl][:, 2048:3072], in0=X[sl][:, 1024 * g:1024 * g + 1024],
                    scalar=1.0, in1=X[sl][:, 1024 * g:1024 * g + 1024],
                    op0=ALU.mult, op1=ALU.mult,
                    accum_out=sumsqV[:, g:g + 1]),
                    reads=[R_X[sl][g]], writes=[R_xh[sl][2], R_sumsqv[sl]])
            if not folded:
                for g in range(ns, NG):
                    emit_sq_v(g)
            for g in range(NG):
                if not folded:
                    S.emit("vector", lambda e, g=g, sl=sl, sums=sums: e.tensor_scalar(
                        out=xh[sl][:, 0:1024], in0=X[sl][:, 1024 * g:1024 * g + 1024],
                        scalar1=1.0, scalar2=0.0, op0=ALU.mult, op1=ALU.add,
                        accum_out=sums[:, g:g + 1]),
                        reads=[R_X[sl][g]], writes=[R_xh[sl][0], R_sums[sl]])
                if g >= ns:
                    if folded:
                        emit_sq_v(g)
                else:
                    S.emit("scalar", lambda e, g=g, sl=sl, sumsqS=sumsqS: e.activation(
                        out=xh[sl][:, 1024:2048], in_=X[sl][:, 1024 * g:1024 * g + 1024],
                        func=AF.Square, accum_out=sumsqS[:, g:g + 1]),
                        reads=[R_X[sl][g]], writes=[R_xh[sl][1], R_sumsq[sl]])
            if folded:
                S.emit("vector", lambda e, sl=sl, sums=sums: e.tensor_tensor(
                    out=sums,
                    in0=xsums[:, 8 * sl:8 * sl + 8].rearrange("p (g f) -> p g f", f=2)[:, :, 0],
                    in1=xsums[:, 8 * sl:8 * sl + 8].rearrange("p (g f) -> p g f", f=2)[:, :, 1],
                    op=ALU.add),
                    reads=[R_xsums[sl]], writes=[R_sums[sl]])
            S.emit("vector", lambda e, sums=sums, mu=mu: e.tensor_scalar(
                out=mu, in0=sums, scalar1=INV_C, scalar2=None, op0=ALU.mult),
                reads=[R_sums[sl]], writes=[R_mu[sl]])
            S.emit("vector", lambda e, mu=mu, musq=musq: e.scalar_tensor_tensor(
                out=musq, in0=mu, scalar=1.0, in1=mu, op0=ALU.mult, op1=ALU.mult),
                reads=[R_mu[sl]], writes=[R_musq[sl]])
            S.emit("vector", lambda e, sumsqS=sumsqS, musq=musq, varr=varr, ns=ns: e.scalar_tensor_tensor(
                out=varr[:, 0:ns], in0=sumsqS[:, 0:ns], scalar=INV_C, in1=musq[:, 0:ns],
                op0=ALU.mult, op1=ALU.subtract),
                reads=[R_sumsq[sl], R_musq[sl]], writes=[R_varr[sl]])
            S.emit("vector", lambda e, sumsqV=sumsqV, musq=musq, varr=varr, ns=ns: e.scalar_tensor_tensor(
                out=varr[:, ns:NG], in0=sumsqV[:, ns:NG], scalar=INV_C, in1=musq[:, ns:NG],
                op0=ALU.mult, op1=ALU.subtract),
                reads=[R_sumsqv[sl], R_musq[sl]], writes=[R_varr[sl]])
            S.emit("scalar", lambda e, varr=varr, stdv=stdv: e.activation(out=stdv, in_=varr, func=AF.Ln,
                                                                          bias=epsb[:, :]),
                   reads=[R_varr[sl], R_epsb], writes=[R_stdv[sl]])
            S.emit("scalar", lambda e, stdv=stdv, rr=rr: e.activation(out=rr, in_=stdv, func=AF.Exp, scale=-0.5),
                   reads=[R_stdv[sl]], writes=[R_rr[sl]])
            S.emit("vector", lambda e, mu=mu, rr=rr, nmur=nmur: e.scalar_tensor_tensor(
                out=nmur, in0=mu, scalar=-1.0, in1=rr, op0=ALU.mult, op1=ALU.mult),
                reads=[R_mu[sl], R_rr[sl]], writes=[R_nmur[sl]])
            for g in range(NG):
                if g % 2 == 0:
                    S.emit("scalar", lambda e, g=g, sl=sl, rr=rr, nmur=nmur: e.activation(
                        out=xh[sl][:, 1024 * g:1024 * g + 1024], in_=X[sl][:, 1024 * g:1024 * g + 1024],
                        func=AF.Identity, scale=rr[:, g:g + 1], bias=nmur[:, g:g + 1]),
                        reads=[R_X[sl][g], R_rr[sl], R_nmur[sl]], writes=[R_xh[sl][g]])
                else:
                    S.emit("vector", lambda e, g=g, sl=sl, rr=rr, nmur=nmur: e.tensor_scalar(
                        out=xh[sl][:, 1024 * g:1024 * g + 1024], in0=X[sl][:, 1024 * g:1024 * g + 1024],
                        scalar1=rr[:, g:g + 1], scalar2=nmur[:, g:g + 1], op0=ALU.mult, op1=ALU.add),
                        reads=[R_X[sl][g], R_rr[sl], R_nmur[sl]], writes=[R_xh[sl][g]])

        def transpose_xh(sl, banks):
            for j in range(8):
                b = banks[j % 2]
                for g in range(NG):
                    S.emit("tensor", lambda e, j=j, g=g, b=b, sl=sl: e.matmul(
                        pb[b][:, 128 * g:128 * g + 128],
                        lhsT=xh[sl][:, 1024 * g + 128 * j:1024 * g + 128 * j + 128],
                        rhs=ident[:], start=(g == 0), stop=(g == 3), skip_group_check=True),
                        chain=(g != 0), reads=[R_xh[sl][g], R_id], writes=[R_pb[b]])
                if j % 2 == 0:
                    S.emit("vector", lambda e, j=j, b=b, sl=sl: e.tensor_copy(
                        out=xf[sl][:, 512 * j:512 * j + 512], in_=pb[b][:, :]),
                        reads=[R_pb[b]], writes=[R_xf[sl][j]])
                else:
                    S.emit("scalar", lambda e, j=j, b=b, sl=sl: e.activation(
                        out=xf[sl][:, 512 * j:512 * j + 512], in_=pb[b][:, :], func=AF.Identity),
                        reads=[R_pb[b]], writes=[R_xf[sl][j]])

        def ffn(sl, banks, pump_fn=None, part=None):
            b0, b1 = banks
            hsb, R_h = qsb[sl], R_q[sl]
            if part == 1:
                _ffn_part2(sl, banks, hsb, R_h, b0, pump_fn)
                return
            for j in range(8):
                S.emit("tensor", lambda e, j=j, sl=sl, b0=b0: e.matmul(
                    pb[b0][:, :], lhsT=win_sb[:, 128 * j:128 * j + 128],
                    rhs=xf[sl][:, 512 * j:512 * j + 512], start=(j == 0), stop=(j == 7)),
                    chain=(j != 0), reads=[R_xf[sl][j], R_win], writes=[R_pb[b0]])
            S.emit("scalar", lambda e, sl=sl, b0=b0: e.activation(
                out=ssbf[sl][:, :], in_=pb[b0][0:64, :], func=AF.Identity, bias=binb[:, :]),
                reads=[R_pb[b0], R_binb], writes=[R_ssbf[sl]])
            if pump_fn:
                pump_fn()
            for j in range(8):
                b = banks[j % 2]
                S.emit("tensor", lambda e, j=j, b=b, sl=sl: e.matmul(
                    pb[b][:, :], lhsT=bd1_sb[:, 128 * j:128 * j + 128],
                    rhs=ssbf[sl][:, :], start=True, stop=True),
                    reads=[R_ssbf[sl], R_bd1], writes=[R_pb[b]])
                S.emit("scalar", lambda e, j=j, b=b, hsb=hsb: e.activation(
                    out=hsb[:, 512 * j:512 * j + 512], in_=pb[b][:, :],
                    func=AF.Relu, bias=b1f[:, j:j + 1]),
                    reads=[R_pb[b], R_b1f], writes=[R_h[j]])
            if part == 0:
                return
            _ffn_part2(sl, banks, hsb, R_h, b0, pump_fn)

        def _ffn_part2(sl, banks, hsb, R_h, b0, pump_fn=None):
            for j in range(8):
                S.emit("tensor", lambda e, j=j, sl=sl, b0=b0, hsb=hsb: e.matmul(
                    pb[b0][:, :], lhsT=bd2_sb[:, 128 * j:128 * j + 128],
                    rhs=hsb[:, 512 * j:512 * j + 512], start=(j == 0), stop=(j == 7)),
                    chain=(j != 0), reads=[R_h[j], R_bd2], writes=[R_pb[b0]])
            S.emit("scalar", lambda e, sl=sl, b0=b0: e.activation(
                out=gsb[sl][0:64, :], in_=pb[b0][0:64, :], func=AF.Identity, bias=b2b[:, :]),
                reads=[R_pb[b0], R_b2b], writes=[R_gsb[sl]])
            if pump_fn:
                pump_fn()
            for g in range(NG):
                for f in range(2):
                    b = banks[(2 * g + f) % 2]
                    S.emit("tensor", lambda e, g=g, f=f, b=b, sl=sl: e.matmul(
                        pb[b][:, :], lhsT=gsb[sl][:, 128 * g:128 * g + 128],
                        rhs=woa_sb[:, 512 * f:512 * f + 512], start=True, stop=True),
                        reads=[R_gsb[sl], R_woa], writes=[R_pb[b]])
                    S.emit("vector", lambda e, g=g, f=f, b=b, sl=sl: e.scalar_tensor_tensor(
                        out=X[sl][:, 1024 * g + 512 * f:1024 * g + 512 * f + 512],
                        in0=pb[b][:, :], scalar=1.0,
                        in1=X[sl][:, 1024 * g + 512 * f:1024 * g + 512 * f + 512],
                        op0=ALU.mult, op1=ALU.add,
                        accum_out=xsums[:, 8 * sl + 2 * g + f:8 * sl + 2 * g + f + 1]),
                        reads=[R_pb[b], R_X[sl][g]], writes=[R_X[sl][g], R_xsums[sl]])

        def qkv_qk2(sl, banks, grp):
            for k16 in range(2 * grp, 2 * grp + 2):
                isq = k16 < 8
                j = k16 % 8
                b = banks[k16 % 2]
                dst = qsb[sl] if isq else ksb[sl]
                rdst = R_q[sl][j] if isq else R_k[sl][j]
                for c in range(8):
                    S.emit("tensor", lambda e, k16=k16, b=b, c=c, sl=sl: e.matmul(
                        pb[b][:, :],
                        lhsT=wa_sb[:, 3072 * c + 128 * k16:3072 * c + 128 * k16 + 128],
                        rhs=xf[sl][:, 512 * c:512 * c + 512], start=(c == 0), stop=(c == 7)),
                        chain=(c != 0), reads=[R_xf[sl][c], R_wa], writes=[R_pb[b]])
                S.emit("scalar", lambda e, b=b: e.activation(out=etmp, in_=pb[b][:, :], func=AF.Exp),
                       reads=[R_pb[b]], writes=[R_etmp])
                S.emit("scalar", lambda e, b=b: e.activation(out=rtmp, in_=pb[b][:, :], func=AF.Relu),
                       reads=[R_pb[b]], writes=[R_rtmp])
                S.emit("vector", lambda e, j=j, dst=dst: e.scalar_tensor_tensor(
                    out=dst[:, 512 * j:512 * j + 512], in0=etmp, scalar=1.0,
                    in1=rtmp, op0=ALU.min, op1=ALU.add),
                    reads=[R_etmp, R_rtmp], writes=[rdst])

        def qkv_v(sl, banks, vg):
            for g in (vg,):
                for f in range(2):
                    b = banks[(2 * g + f) % 2]
                    for c in range(8):
                        S.emit("tensor", lambda e, g=g, f=f, b=b, c=c, sl=sl: e.matmul(
                            pb[b][:, :],
                            lhsT=xf[sl][:, 512 * c + 128 * g:512 * c + 128 * g + 128],
                            rhs=wa_sb[:, 3072 * c + 2048 + 512 * f:3072 * c + 2048 + 512 * f + 512],
                            start=(c == 0), stop=(c == 7)),
                            chain=(c != 0), reads=[R_xf[sl][c], R_wa], writes=[R_pb[b]])
                    dstap = vsb[sl][:, 1056 * g + 528 * f:1056 * g + 528 * f + 528] \
                        .rearrange("p (h c) -> p h c", c=66)[:, :, 0:64]
                    S.emit("vector", lambda e, b=b, dstap=dstap: e.tensor_copy(
                        out=dstap, in_=pb[b][:, :].rearrange("p (h c) -> p h c", c=64)),
                        reads=[R_pb[b]], writes=[R_v[sl][g]])

        def scan_p1(sl, t, c):
            for j in range(8):
                b = j // 4
                S.emit("tensor", lambda e, j=j, c=c, b=b, sl=sl: e.matmul(
                    pb[b][:, 128 * (j % 4):128 * (j % 4) + 128],
                    lhsT=ksb[sl][:, 512 * j + 128 * c:512 * j + 128 * c + 128],
                    rhs=ident[:], start=(j % 4 == 0), stop=(j % 4 == 3), skip_group_check=True),
                    chain=(j % 4 != 0), reads=[R_k[sl][j], R_id], writes=[R_pb[b]])
            for b in range(2):
                S.emit("vector", lambda e, b=b: e.tensor_copy(
                    out=ktmc[:, 512 * b:512 * b + 512], in_=pb[b][:, :]),
                    reads=[R_pb[b]], writes=[R_ktmc[b]])
            mb_ = mask4[:, :].unsqueeze(1).broadcast_to([128, 4, 128])
            for a2 in range(2):
                bs = 2 - 2 * a2
                for hh in (0, 2, 4, 6, 1, 3, 5, 7):
                    h = 8 * a2 + hh
                    b = bs + hh % 2
                    po = 64 * (h % 2)
                    j = h // 2
                    S.emit("tensor", lambda e, h=h, b=b, po=po, j=j, c=c, hh=hh, sl=sl: e.matmul(
                        pb[b][:, 128 * (hh // 2):128 * (hh // 2) + 128],
                        lhsT=ksb[sl][po:po + 64, 512 * j + 128 * c:512 * j + 128 * c + 128],
                        rhs=qsb[sl][po:po + 64, 512 * j + 128 * c:512 * j + 128 * c + 128],
                        start=(hh in (0, 1)), stop=(hh in (6, 7)), skip_group_check=True),
                        chain=(hh not in (0, 1)), reads=[R_k[sl][j], R_q[sl][j]], writes=[R_pb[b]])
                for bb in range(2):
                    blk = 2 * a2 + bb
                    S.emit("vector", lambda e, bb=bb, blk=blk, bs=bs, mb_=mb_: e.tensor_tensor(
                        out=atsb[:, 512 * blk:512 * blk + 512].rearrange("p (h c) -> p h c", c=128),
                        in0=pb[bs + bb][:, :].rearrange("p (h c) -> p h c", c=128),
                        in1=mb_, op=ALU.mult),
                        reads=[R_pb[bs + bb], R_mask], writes=[R_at[blk]])

        def scan_p2(sl, t, c):
            gc = (t % 4) * 4 + c
            first = gc == 0
            for a2 in range(2):
                bs = 2 - 2 * a2
                for hh in range(8):
                    h = 8 * a2 + hh
                    b = bs + hh // 4
                    col = 65 * (hh % 4)
                    po = 64 * (h % 2)
                    j = h // 2
                    if not first:
                        S.emit("tensor", lambda e, h=h, b=b, col=col, po=po, j=j, c=c, hh=hh, sl=sl: e.matmul(
                            pb[b][:, col:col + 65],
                            lhsT=qsb[sl][po:po + 64, 512 * j + 128 * c:512 * j + 128 * c + 128],
                            rhs=Ssb[po:po + 64, 65 * (h // 2):65 * (h // 2) + 65],
                            start=(hh % 4 == 0), stop=False, skip_group_check=True),
                            chain=(hh % 4 != 0), reads=[R_q[sl][j], R_Ssb[h // 8]], writes=[R_pb[b]])
                    ao = 1024 * (h // 8) + 512 * (h % 2) + 128 * ((h % 8) // 2)
                    S.emit("tensor", lambda e, h=h, b=b, col=col, ao=ao, c=c, hh=hh, first=first, sl=sl: e.matmul(
                        pb[b][:, col:col + 65],
                        lhsT=atsb[:, ao:ao + 128],
                        rhs=vsb[sl][:, 1056 * c + 66 * h:1056 * c + 66 * h + 65],
                        start=(first and hh % 4 == 0), stop=True, skip_group_check=True),
                        chain=(not (first and hh % 4 == 0)), reads=[R_at[2 * (h // 8) + h % 2], R_v[sl][c]], writes=[R_pb[b]])
                for bb in range(2):
                    base = 8 * a2 + 4 * bb
                    src = pb[bs + bb][:, 0:260].rearrange("p (h c) -> p h c", c=65)[:, :, 64]
                    S.emit("vector", lambda e, base=base, src=src, bs=bs, bb=bb: e.tensor_scalar(
                        out=dgat[:, base:base + 4], in0=src, scalar1=EPS_AT, scalar2=None, op0=ALU.add),
                        reads=[R_pb[bs + bb]], writes=[R_dgat])
                S.emit("vector", lambda e, a2=a2: e.reciprocal(
                    out=rcp[:, 8 * a2:8 * a2 + 8], in_=dgat[:, 8 * a2:8 * a2 + 8]),
                    reads=[R_dgat], writes=[R_rcp])
                for bb in range(2):
                    base = 8 * a2 + 4 * bb
                    src = pb[bs + bb][:, 0:260].rearrange("p (h c) -> p h c", c=65)[:, :, 0:64]
                    rb = rcp[:, base:base + 4].unsqueeze(2).broadcast_to([128, 4, 64])
                    dst = ytm[:, 64 * base:64 * base + 256].rearrange("p (h c) -> p h c", c=64)
                    S.emit("vector", lambda e, src=src, rb=rb, dst=dst, bs=bs, bb=bb: e.tensor_tensor(
                        out=dst, in0=src, in1=rb, op=ALU.mult),
                        reads=[R_pb[bs + bb], R_rcp], writes=[R_ytm[a2]])

        def scan_p3(sl, t, c):
            gc = (t % 4) * 4 + c
            first = gc == 0
            if first:
                for b in (5, 6):
                    S.emit("tensor", lambda e, b=b: e.matmul(
                        pb[b][:, 0:260], lhsT=ident[:], rhs=zsb,
                        start=True, stop=False, skip_group_check=True),
                        reads=[R_id, R_zsb], writes=[R_pb[b]])
            for h in range(16):
                p2 = h // 2
                b = 5 + p2 // 4
                col = 65 * (p2 % 4)
                po = 64 * (h % 2)
                S.emit("tensor", lambda e, h=h, b=b, col=col, po=po, c=c, sl=sl: e.matmul(
                    pb[b][po:po + 64, col:col + 65],
                    lhsT=ktmc[:, 64 * h:64 * h + 64],
                    rhs=vsb[sl][:, 1056 * c + 66 * h:1056 * c + 66 * h + 65],
                    start=False, stop=True, skip_group_check=True,
                    tile_position=(0, po)),
                    chain=True, reads=[R_ktmc[h // 8], R_v[sl][c]], writes=[R_pb[b]])
            S.emit("vector", lambda e: e.tensor_copy(out=Ssb[:, 0:260], in_=pb[5][:, 0:260]),
                   reads=[R_pb[5]], writes=[R_Ssb[0]])
            S.emit("vector", lambda e: e.tensor_copy(out=Ssb[:, 260:520], in_=pb[6][:, 0:260]),
                   reads=[R_pb[6]], writes=[R_Ssb[1]])
            for j in range(8):
                b = 2 + j // 4
                S.emit("tensor", lambda e, j=j, b=b: e.matmul(
                    pb[b][:, 128 * (j % 4):128 * (j % 4) + 128],
                    lhsT=ytm[:, 128 * j:128 * j + 128],
                    rhs=ident[:], start=(j % 4 == 0), stop=(j % 4 == 3), skip_group_check=True),
                    chain=(j % 4 != 0), reads=[R_ytm[j // 4], R_id], writes=[R_pb[b]])
            for b in range(2):
                S.emit("vector", lambda e, b=b: e.tensor_copy(
                    out=yfc[:, 512 * b:512 * b + 512], in_=pb[2 + b][:, :]),
                    reads=[R_pb[2 + b]], writes=[R_yfc[b]])
            for f in range(2):
                b = f
                for j in range(8):
                    S.emit("tensor", lambda e, f=f, b=b, j=j: e.matmul(
                        pb[b][:, :],
                        lhsT=yfc[:, 128 * j:128 * j + 128],
                        rhs=wp_sb[:, 1024 * j + 512 * f:1024 * j + 512 * f + 512],
                        start=(j == 0), stop=(j == 7)),
                        chain=(j != 0), reads=[R_yfc[j // 4], R_wp], writes=[R_pb[b]])
                S.emit("vector", lambda e, f=f, b=b, c=c, sl=sl: e.scalar_tensor_tensor(
                    out=X[sl][:, 1024 * c + 512 * f:1024 * c + 512 * f + 512],
                    in0=pb[b][:, :], scalar=1.0,
                    in1=X[sl][:, 1024 * c + 512 * f:1024 * c + 512 * f + 512],
                    op0=ALU.mult, op1=ALU.add,
                    accum_out=xsums[:, 8 * sl + 2 * c + f:8 * sl + 2 * c + f + 1]),
                    reads=[R_pb[b], R_X[sl][c]], writes=[R_X[sl][c], R_xsums[sl]])

        FRONT_BANKS = (4, 7)
        BACK_BANKS23 = (2, 3)

        def transpose_xh_half(sl, banks, hlf):
            for g in (2 * hlf, 2 * hlf + 1):
                for j in range(8):
                    b = banks[j // 4]
                    S.emit("tensor", lambda e, j=j, g=g, b=b, sl=sl: e.matmul(
                        pb[b][:, 128 * (j % 4):128 * (j % 4) + 128],
                        lhsT=xh[sl][:, 1024 * g + 128 * j:1024 * g + 128 * j + 128],
                        rhs=ident[:], start=(j % 4 == 0), stop=(j % 4 == 3), skip_group_check=True),
                        chain=(j % 4 != 0), reads=[R_xh[sl][g], R_id], writes=[R_pb[b]])
                for bb in range(2):
                    dst = xf[sl][:, 2048 * bb:2048 * bb + 2048].rearrange(
                        "p (j t) -> p j t", t=512)[:, :, 128 * g:128 * g + 128]
                    S.emit("scalar", lambda e, g=g, bb=bb, sl=sl, dst=dst: e.activation(
                        out=dst, in_=pb[banks[bb]][:, :].rearrange("p (j t) -> p j t", t=128),
                        func=AF.Identity),
                        reads=[R_pb[banks[bb]]],
                        writes=[R_xf[sl][4 * bb], R_xf[sl][4 * bb + 1], R_xf[sl][4 * bb + 2], R_xf[sl][4 * bb + 3]])

        def front_sections(t):
            sl = t % 2
            seq, tl = t // 4, t % 4
            secs = []
            src = x_d[seq, 512 * tl:512 * tl + 512, :].rearrange("(g p) c -> p g c", p=128)
            secs.append(lambda src=src, sl=sl: S.emit("sync", lambda e, src=src, sl=sl: e.dma_start(
                out=X[sl][:, :].rearrange("p (g c) -> p g c", c=1024), in_=src),
                writes=list(R_X[sl]), dma=True))
            secs.append(lambda sl=sl: ln_pass(sl, FRONT_BANKS))
            secs.append(lambda sl=sl: transpose_xh_half(sl, FRONT_BANKS, 0))
            secs.append(lambda sl=sl: transpose_xh_half(sl, FRONT_BANKS, 1))
            secs.append(lambda sl=sl: ffn(sl, FRONT_BANKS, part=0))
            secs.append(lambda sl=sl: ffn(sl, FRONT_BANKS, part=1))
            secs.append(lambda sl=sl: ln_pass(sl, FRONT_BANKS, folded=True))
            secs.append(lambda sl=sl: transpose_xh_half(sl, FRONT_BANKS, 0))
            secs.append(lambda sl=sl: transpose_xh_half(sl, FRONT_BANKS, 1))
            for grp in range(8):
                secs.append(lambda sl=sl, grp=grp: qkv_qk2(sl, FRONT_BANKS, grp))
            for vg in range(4):
                secs.append(lambda sl=sl, vg=vg: qkv_v(sl, FRONT_BANKS, vg))
            secs.append(None)
            return secs

        from collections import deque
        fq = deque()

        def pump(n=1):
            for _ in range(n):
                if fq:
                    s = fq.popleft()
                    if s is not None:
                        s()

        def back(t):
            sl = t % 2
            seq, tl = t // 4, t % 4
            for c in range(4):
                scan_p1(sl, t, c)
                pump()
                scan_p2(sl, t, c)
                pump()
                scan_p3(sl, t, c)
                pump()
            ln_pass(sl, BACK_BANKS23, folded=True)
            pump()
            transpose_xh_half(sl, BACK_BANKS23, 0)
            pump()
            transpose_xh_half(sl, BACK_BANKS23, 1)
            pump()
            ffn(sl, BACK_BANKS23, pump_fn=pump)
            pump()
            dst = out_d[seq, 512 * tl:512 * tl + 512, :].rearrange("(g p) c -> p g c", p=128)
            S.emit("gpsimd", lambda e, dst=dst, sl=sl: e.dma_start(
                out=dst, in_=X[sl][:, :].rearrange("p (g c) -> p g c", c=1024)),
                reads=list(R_X[sl]), dma=True, force_inc=True)

        fq.extend(front_sections(0))
        while fq:
            s = fq.popleft()
            if s is not None:
                s()
        for t in range(NTILE):
            if t + 1 < NTILE:
                fq.extend(front_sections(t + 1))
            back(t)
            while fq:
                s = fq.popleft()
                if s is not None:
                    s()

        run_engine = S.replay(sems)

        @block.sync
        def _(sync):
            run_engine(sync, "sync")

        @block.tensor
        def _(tensor):
            run_engine(tensor, "tensor")

        @block.vector
        def _(vector):
            run_engine(vector, "vector")

        @block.scalar
        def _(scalar):
            run_engine(scalar, "scalar")

        @block.gpsimd
        def _(gpsimd):
            run_engine(gpsimd, "gpsimd")
            gpsimd.wait_ge(sems["gpsimd"], S.final_cnt["gpsimd"])

    return nc


_NC = None


def _get_nc():
    global _NC
    if _NC is None:
        _NC = build_program()
    return _NC


def kernel(L0_attn_x, L0_attn_w_attn, L0_attn_w_proj, L0_ffn_Win, L0_ffn_bin_,
           L0_ffn_W1, L0_ffn_b1, L0_ffn_W2, L0_ffn_b2, L0_ffn_Wout, L0_ffn_bout):
    x = np.ascontiguousarray(np.asarray(L0_attn_x, dtype=np.float32))
    wa = np.ascontiguousarray(np.asarray(L0_attn_w_attn, dtype=np.float32))
    wp = np.ascontiguousarray(np.asarray(L0_attn_w_proj, dtype=np.float32))
    win = np.asarray(L0_ffn_Win, dtype=np.float32)
    binv = np.asarray(L0_ffn_bin_, dtype=np.float32)
    W1 = np.asarray(L0_ffn_W1, dtype=np.float32)
    b1 = np.asarray(L0_ffn_b1, dtype=np.float32)
    W2 = np.asarray(L0_ffn_W2, dtype=np.float32)
    b2 = np.asarray(L0_ffn_b2, dtype=np.float32)
    wout = np.ascontiguousarray(np.asarray(L0_ffn_Wout, dtype=np.float32))
    bout = np.asarray(L0_ffn_bout, dtype=np.float32)

    win_r = np.ascontiguousarray(
        np.concatenate([win.reshape(8, 128, 64)] * 2, axis=2).transpose(1, 0, 2).reshape(128, 1024))
    bd1 = np.zeros((64, 1024), np.float32)
    m_idx = np.arange(64)
    for i in range(16):
        bd1[m_idx, 16 * m_idx + i] = W1[:, i]
    bd2 = np.zeros((1024, 64), np.float32)
    for i in range(16):
        bd2[16 * m_idx + i, m_idx] = W2[:, i]
    bd2_r = np.ascontiguousarray(
        np.concatenate([bd2.reshape(8, 128, 64)] * 2, axis=2).transpose(1, 0, 2).reshape(128, 1024))
    b1f = np.ascontiguousarray(b1.reshape(1024).reshape(8, 128).T)
    binb = np.ascontiguousarray(binv.reshape(64, 1))
    b2b = np.ascontiguousarray(b2.reshape(64, 1))
    bout_r = np.ascontiguousarray(bout.reshape(1, 1024))
    ident = np.eye(128, dtype=ml_dtypes.bfloat16)
    mask4 = np.ascontiguousarray(np.triu(np.ones((128, 128), np.float32)).astype(ml_dtypes.bfloat16))

    shared = dict(w_attn=wa, w_proj=wp, win_r=win_r, bd1=bd1, bd2_r=bd2_r,
                  wout=wout, bout_r=bout_r, binb=binb, b1f=b1f, b2b=b2b,
                  ident=ident, mask4=mask4)
    in_maps = []
    for i in range(NCORE):
        m = dict(shared)
        m["x"] = np.ascontiguousarray(x[2 * i:2 * i + 2])
        in_maps.append(m)

    nc = _get_nc()
    res = run_bass_kernel_spmd(nc, in_maps, core_ids=list(range(NCORE)))
    out = np.concatenate([np.asarray(res.results[i]["out"]) for i in range(NCORE)], axis=0)
    return out


# version 53
# speedup vs baseline: 1.0060x; 1.0060x over previous
import numpy as np
import ml_dtypes
import concourse.bass as bass
import concourse.mybir as mybir
from concourse.bass_utils import run_bass_kernel_spmd

F32 = mybir.dt.float32
BF16 = mybir.dt.bfloat16
AF = mybir.ActivationFunctionType
ALU = mybir.AluOpType

B, T, C = 16, 2048, 1024
NCORE = 8
H, D = 16, 64
TILE = 512
NTILE = 8
NG = 4
EPS_LN = 1e-5
EPS_AT = 1e-6
INV_C = 1.0 / 1024.0

ENGS = ("sync", "tensor", "vector", "scalar", "gpsimd")


class Res:
    __slots__ = ("w", "r")

    def __init__(self):
        self.w = []
        self.r = []


class Sched:

    def __init__(self):
        self.ops = []
        self.forced = set()
        self.chained = set()

    def emit(self, eng, fn, reads=(), writes=(), deps=(), dma=False, force_inc=False, chain=False):
        deps = set(deps)
        for r in reads:
            deps.update(r.w)
        for r in writes:
            deps.update(r.w)
            deps.update(r.r)
        i = len(self.ops)
        self.ops.append((eng, fn, sorted(deps), 16 if dma else 1))
        if chain:
            self.chained.add(i)
        if force_inc or dma:
            self.forced.add(i)
        for r in reads:
            r.r.append(i)
        for r in writes:
            r.w = [i]
            r.r = []
        return i

    def replay(self, sems):
        n = len(self.ops)
        need = [False] * n
        for i in self.forced:
            need[i] = True
        self_sync = {}
        prev_own = {}
        for i, (eng, fn, deps, amt) in enumerate(self.ops):
            p = prev_own.get(eng)
            if p is not None and p in deps and i not in self.chained:
                self_sync[i] = p
                need[p] = True
            prev_own[eng] = i
        for eng, fn, deps, amt in self.ops:
            for d in deps:
                if self.ops[d][0] != eng:
                    need[d] = True
        val = [0] * n
        cnt = {e: 0 for e in ENGS}
        for i, (eng, fn, deps, amt) in enumerate(self.ops):
            if need[i]:
                cnt[eng] += amt
                val[i] = cnt[eng]
        self.final_cnt = cnt

        def run_engine(engine_obj, ename):
            waited = {e: 0 for e in ENGS}
            own_dma = 0
            for i, (eng, fn, deps, amt) in enumerate(self.ops):
                if eng != ename:
                    continue
                if amt == 16:
                    if own_dma > 0 and waited[ename] < 16 * own_dma:
                        engine_obj.wait_ge(sems[ename], 16 * own_dma)
                        waited[ename] = 16 * own_dma
                    own_dma += 1
                want = {}
                for d in deps:
                    de = self.ops[d][0]
                    if de == ename:
                        continue
                    v = val[d]
                    assert v > 0, f"dep {d} of op {i} has no inc"
                    if v > want.get(de, 0):
                        want[de] = v
                if i in self_sync:
                    v = val[self_sync[i]]
                    if v > want.get(ename, 0):
                        want[ename] = v
                for de, v in want.items():
                    if waited[de] < v:
                        engine_obj.wait_ge(sems[de], v)
                        waited[de] = v
                ins = fn(engine_obj)
                if need[i]:
                    ins.then_inc(sems[eng], amt)
        return run_engine


def build_program():
    nc = bass.Bass("TRN2", target_bir_lowering=False)

    x_d = nc.dram_tensor("x", [2, T, C], F32, kind="ExternalInput").ap()
    out_d = nc.dram_tensor("out", [2, T, C], F32, kind="ExternalOutput").ap()
    wa_d = nc.dram_tensor("w_attn", [C, 3 * C], F32, kind="ExternalInput").ap()
    wp_d = nc.dram_tensor("w_proj", [C, C], F32, kind="ExternalInput").ap()
    win_d = nc.dram_tensor("win_r", [128, 1024], F32, kind="ExternalInput").ap()
    bd1_d = nc.dram_tensor("bd1", [64, 1024], F32, kind="ExternalInput").ap()
    bd2_d = nc.dram_tensor("bd2_r", [128, 1024], F32, kind="ExternalInput").ap()
    wout_d = nc.dram_tensor("wout", [64, 1024], F32, kind="ExternalInput").ap()
    bout_d = nc.dram_tensor("bout_r", [1, 1024], F32, kind="ExternalInput").ap()
    binb_d = nc.dram_tensor("binb", [64, 1], F32, kind="ExternalInput").ap()
    b1f_d = nc.dram_tensor("b1f", [128, 8], F32, kind="ExternalInput").ap()
    b2b_d = nc.dram_tensor("b2b", [64, 1], F32, kind="ExternalInput").ap()
    id_d = nc.dram_tensor("ident", [128, 128], BF16, kind="ExternalInput").ap()
    mask_d = nc.dram_tensor("mask4", [128, 128], BF16, kind="ExternalInput").ap()

    from contextlib import ExitStack
    with ExitStack() as _ctx:
        def sb(name, shape, dt):
            return _ctx.enter_context(nc.sbuf_tensor(name, shape, dt))
        wa_sb = sb("wa_sb", [128, 8 * 3072], BF16)
        wp_sb = sb("wp_sb", [128, 8 * 1024], BF16)
        win_sb = sb("win_sb", [128, 1024], BF16)
        bd1_sb = sb("bd1_sb", [64, 1024], BF16)
        bd2_sb = sb("bd2_sb", [128, 1024], BF16)
        woa_sb = sb("woa_sb", [65, 1024], BF16)
        ident = sb("ident_sb", [128, 128], BF16)
        mask4 = sb("mask4_sb", [128, 128], BF16)
        binb = sb("binb_sb", [64, 1], F32)
        b1f = sb("b1f_sb", [128, 8], F32)
        b2b = sb("b2b_sb", [64, 1], F32)
        stgA = sb("stgA", [128, 1536], F32)
        stgB = sb("stgB", [128, 1536], F32)
        X = [sb("X0", [128, 4096], F32), sb("X1", [128, 4096], F32)]
        xh = [sb("xh0", [128, 4096], BF16), sb("xh1", [128, 4096], BF16)]
        xf = [sb("xf0", [128, 4096], BF16), sb("xf1", [128, 4096], BF16)]
        qsb = [sb("qsb0", [128, 4096], BF16), sb("qsb1", [128, 4096], BF16)]
        ksb = [sb("ksb0", [128, 4096], BF16), sb("ksb1", [128, 4096], BF16)]
        vsb = [sb("vsb0", [128, 4224], BF16), sb("vsb1", [128, 4224], BF16)]
        ssbf = [sb("ssbf0", [64, 512], BF16), sb("ssbf1", [64, 512], BF16)]
        gsb = [sb("gsb0", [65, 512], BF16), sb("gsb1", [65, 512], BF16)]
        stats = sb("stats", [128, 64], F32)
        xsums = sb("xsums", [128, 16], F32)
        epsb = sb("epsb", [128, 1], F32)
        atsb = sb("atsb", [128, 2048], BF16)
        Ssb = sb("Ssb", [128, 520], BF16)
        dgat = sb("dgat", [128, 16], F32)
        rcp = sb("rcp", [128, 16], F32)
        stgA_bf = stgA[:, :].bitcast(BF16)
        stgB_bf = stgB[:, :].bitcast(BF16)
        etmp = stgA_bf[:, 0:512]
        rtmp = stgA_bf[:, 512:1024]
        yfc = stgA_bf[:, 1024:2048]
        ktmc = stgB_bf[:, 0:1024]
        zsb = stgB_bf[:, 1024:1284]
        ytm = stgB_bf[:, 1284:2308]
        pb = [_ctx.enter_context(nc.psum_tensor(f"pb{i}", [128, 512], F32)) for i in range(8)]
        sems = {e: _ctx.enter_context(nc.semaphore(f"sem_{e}")) for e in ENGS}
        block = _ctx.enter_context(nc.Block())

        S = Sched()
        R_pb = [Res() for _ in range(8)]
        R_stgA, R_stgB = Res(), Res()
        R_wa, R_wp, R_win, R_bd1, R_bd2, R_woa = (Res() for _ in range(6))
        R_id, R_mask, R_binb, R_b1f, R_b2b, R_zsb = (Res() for _ in range(6))
        R_X = [[Res() for _ in range(NG)] for _ in range(2)]
        R_xh = [[Res() for _ in range(NG)] for _ in range(2)]
        R_xf = [[Res() for _ in range(8)] for _ in range(2)]
        R_q = [[Res() for _ in range(8)] for _ in range(2)]
        R_k = [[Res() for _ in range(8)] for _ in range(2)]
        R_v = [[Res() for _ in range(NG)] for _ in range(2)]
        R_ssbf = [Res(), Res()]
        R_gsb = [Res(), Res()]
        R_ktmc = [Res(), Res()]
        R_at = [Res() for _ in range(4)]
        R_Ssb = [Res(), Res()]
        R_ytm = [Res(), Res()]
        R_yfc = [Res(), Res()]
        R_etmp, R_rtmp = Res(), Res()
        R_dumpv, R_dumps = Res(), Res()
        R_sums = [Res(), Res()]
        R_sumsq = [Res(), Res()]
        R_mu = [Res(), Res()]
        R_musq = [Res(), Res()]
        R_varr = [Res(), Res()]
        R_stdv = [Res(), Res()]
        R_rr = [Res(), Res()]
        R_nmur = [Res(), Res()]
        R_dgat, R_rcp = Res(), Res()
        R_xsums = [Res(), Res()]
        R_epsb = Res()

        S.emit("sync", lambda e: e.dma_start(out=ident[:], in_=id_d), writes=[R_id], dma=True)
        S.emit("sync", lambda e: e.dma_start(out=mask4[:], in_=mask_d), writes=[R_mask], dma=True)
        S.emit("sync", lambda e: e.dma_start(out=binb[:], in_=binb_d), writes=[R_binb], dma=True)
        S.emit("sync", lambda e: e.dma_start(out=b1f[:], in_=b1f_d), writes=[R_b1f], dma=True)
        S.emit("sync", lambda e: e.dma_start(out=b2b[:], in_=b2b_d), writes=[R_b2b], dma=True)

        S.emit("vector", lambda e: e.memset(epsb[:, :], EPS_LN), writes=[R_epsb])
        for sl in range(2):
            S.emit("vector", lambda e, sl=sl: e.memset(gsb[sl][64:65, :], 1.0), writes=[R_gsb[sl]])
            for g in range(NG):
                ap = vsb[sl][:, 1056 * g:1056 * g + 1056].rearrange("p (h c) -> p h c", c=66)[:, :, 64]
                S.emit("vector", lambda e, ap=ap: e.memset(ap, 1.0), writes=[R_v[sl][g]])

        _stg_flip = [0]

        def staged(src_ap, rows, cols, dst_ap, wres, scale=None):
            stg, rstg = (stgA, R_stgA) if _stg_flip[0] == 0 else (stgB, R_stgB)
            q = "gpsimd" if _stg_flip[0] == 0 else "sync"
            _stg_flip[0] ^= 1
            sl = stg[0:rows, 0:cols]
            S.emit(q, lambda e, s=src_ap, d=sl: e.dma_start(out=d, in_=s),
                   writes=[rstg], dma=True)
            if scale is None:
                S.emit("vector", lambda e, d=sl, o=dst_ap: e.tensor_copy(out=o, in_=d),
                       reads=[rstg], writes=[wres])
            else:
                S.emit("vector", lambda e, d=sl, o=dst_ap: e.tensor_scalar(
                    out=o, in0=d, scalar1=scale, scalar2=None, op0=ALU.mult),
                    reads=[rstg], writes=[wres])

        staged(win_d, 128, 1024, win_sb[:], R_win)
        staged(bd1_d, 64, 1024, bd1_sb[:], R_bd1)
        staged(bd2_d, 128, 1024, bd2_sb[:], R_bd2)
        stg, rstg = (stgA, R_stgA) if _stg_flip[0] == 0 else (stgB, R_stgB)
        _stg_flip[0] ^= 1
        S.emit("gpsimd", lambda e, stg=stg: e.dma_start(out=stg[0:64, 0:1024], in_=wout_d),
               writes=[rstg], dma=True)
        S.emit("gpsimd", lambda e, stg=stg: e.dma_start(out=stg[64:65, 0:1024], in_=bout_d),
               writes=[rstg], dma=True)
        S.emit("vector", lambda e, stg=stg: e.tensor_scalar(
            out=woa_sb[0:65, :], in0=stg[0:65, 0:1024], scalar1=0.5, scalar2=None, op0=ALU.mult),
            reads=[rstg], writes=[R_woa])
        for j in range(8):
            for hlf in range(2):
                staged(wa_d[128 * j:128 * j + 128, 1536 * hlf:1536 * hlf + 1536], 128, 1536,
                       wa_sb[:, 3072 * j + 1536 * hlf:3072 * j + 1536 * hlf + 1536], R_wa)
        for j in range(8):
            staged(wp_d[128 * j:128 * j + 128, :], 128, 1024,
                   wp_sb[:, 1024 * j:1024 * j + 1024], R_wp)
        for r in (R_etmp, R_rtmp, R_yfc[0], R_yfc[1]):
            r.r.extend(R_stgA.r)
            r.w = list(R_stgA.w)
        for r in (R_ktmc[0], R_ktmc[1], R_zsb, R_ytm[0], R_ytm[1]):
            r.r.extend(R_stgB.r)
            r.w = list(R_stgB.w)
        S.emit("vector", lambda e: e.memset(zsb, 0.0), writes=[R_zsb])

        def ln_pass(sl, fb, folded=False):
            o = 32 * sl
            sums = stats[:, o + 0:o + 4]
            sumsq = stats[:, o + 4:o + 8]
            mu = stats[:, o + 8:o + 12]
            musq = stats[:, o + 12:o + 16]
            varr = stats[:, o + 16:o + 20]
            stdv = stats[:, o + 20:o + 24]
            rr = stats[:, o + 24:o + 28]
            nmur = stats[:, o + 28:o + 32]
            for g in range(NG):
                if not folded:
                    S.emit("vector", lambda e, g=g, sl=sl, sums=sums: e.tensor_scalar(
                        out=xh[sl][:, 0:1024], in0=X[sl][:, 1024 * g:1024 * g + 1024],
                        scalar1=1.0, scalar2=0.0, op0=ALU.mult, op1=ALU.add,
                        accum_out=sums[:, g:g + 1]),
                        reads=[R_X[sl][g]], writes=[R_xh[sl][0], R_sums[sl]])
                S.emit("scalar", lambda e, g=g, sl=sl, sumsq=sumsq: e.activation(
                    out=xh[sl][:, 1024:2048], in_=X[sl][:, 1024 * g:1024 * g + 1024],
                    func=AF.Square, accum_out=sumsq[:, g:g + 1]),
                    reads=[R_X[sl][g]], writes=[R_xh[sl][1], R_sumsq[sl]])
            if folded:
                S.emit("vector", lambda e, sl=sl, sums=sums: e.tensor_tensor(
                    out=sums,
                    in0=xsums[:, 8 * sl:8 * sl + 8].rearrange("p (g f) -> p g f", f=2)[:, :, 0],
                    in1=xsums[:, 8 * sl:8 * sl + 8].rearrange("p (g f) -> p g f", f=2)[:, :, 1],
                    op=ALU.add),
                    reads=[R_xsums[sl]], writes=[R_sums[sl]])
            S.emit("vector", lambda e, sums=sums, mu=mu: e.tensor_scalar(
                out=mu, in0=sums, scalar1=INV_C, scalar2=None, op0=ALU.mult),
                reads=[R_sums[sl]], writes=[R_mu[sl]])
            S.emit("vector", lambda e, mu=mu, musq=musq: e.scalar_tensor_tensor(
                out=musq, in0=mu, scalar=1.0, in1=mu, op0=ALU.mult, op1=ALU.mult),
                reads=[R_mu[sl]], writes=[R_musq[sl]])
            S.emit("vector", lambda e, sumsq=sumsq, musq=musq, varr=varr: e.scalar_tensor_tensor(
                out=varr, in0=sumsq, scalar=INV_C, in1=musq, op0=ALU.mult, op1=ALU.subtract),
                reads=[R_sumsq[sl], R_musq[sl]], writes=[R_varr[sl]])
            S.emit("scalar", lambda e, varr=varr, stdv=stdv: e.activation(out=stdv, in_=varr, func=AF.Ln,
                                                                          bias=epsb[:, :]),
                   reads=[R_varr[sl], R_epsb], writes=[R_stdv[sl]])
            S.emit("scalar", lambda e, stdv=stdv, rr=rr: e.activation(out=rr, in_=stdv, func=AF.Exp, scale=-0.5),
                   reads=[R_stdv[sl]], writes=[R_rr[sl]])
            S.emit("vector", lambda e, mu=mu, rr=rr, nmur=nmur: e.scalar_tensor_tensor(
                out=nmur, in0=mu, scalar=-1.0, in1=rr, op0=ALU.mult, op1=ALU.mult),
                reads=[R_mu[sl], R_rr[sl]], writes=[R_nmur[sl]])
            for g in range(NG):
                if g % 2 == 0:
                    S.emit("scalar", lambda e, g=g, sl=sl, rr=rr, nmur=nmur: e.activation(
                        out=xh[sl][:, 1024 * g:1024 * g + 1024], in_=X[sl][:, 1024 * g:1024 * g + 1024],
                        func=AF.Identity, scale=rr[:, g:g + 1], bias=nmur[:, g:g + 1]),
                        reads=[R_X[sl][g], R_rr[sl], R_nmur[sl]], writes=[R_xh[sl][g]])
                else:
                    S.emit("vector", lambda e, g=g, sl=sl, rr=rr, nmur=nmur: e.tensor_scalar(
                        out=xh[sl][:, 1024 * g:1024 * g + 1024], in0=X[sl][:, 1024 * g:1024 * g + 1024],
                        scalar1=rr[:, g:g + 1], scalar2=nmur[:, g:g + 1], op0=ALU.mult, op1=ALU.add),
                        reads=[R_X[sl][g], R_rr[sl], R_nmur[sl]], writes=[R_xh[sl][g]])

        def transpose_xh(sl, banks):
            for j in range(8):
                b = banks[j % 2]
                for g in range(NG):
                    S.emit("tensor", lambda e, j=j, g=g, b=b, sl=sl: e.matmul(
                        pb[b][:, 128 * g:128 * g + 128],
                        lhsT=xh[sl][:, 1024 * g + 128 * j:1024 * g + 128 * j + 128],
                        rhs=ident[:], start=(g == 0), stop=(g == 3), skip_group_check=True),
                        chain=(g != 0), reads=[R_xh[sl][g], R_id], writes=[R_pb[b]])
                if j % 2 == 0:
                    S.emit("vector", lambda e, j=j, b=b, sl=sl: e.tensor_copy(
                        out=xf[sl][:, 512 * j:512 * j + 512], in_=pb[b][:, :]),
                        reads=[R_pb[b]], writes=[R_xf[sl][j]])
                else:
                    S.emit("scalar", lambda e, j=j, b=b, sl=sl: e.activation(
                        out=xf[sl][:, 512 * j:512 * j + 512], in_=pb[b][:, :], func=AF.Identity),
                        reads=[R_pb[b]], writes=[R_xf[sl][j]])

        def ffn(sl, banks, pump_fn=None, part=None):
            b0, b1 = banks
            hsb, R_h = qsb[sl], R_q[sl]
            if part == 1:
                _ffn_part2(sl, banks, hsb, R_h, b0, pump_fn)
                return
            for j in range(8):
                S.emit("tensor", lambda e, j=j, sl=sl, b0=b0: e.matmul(
                    pb[b0][:, :], lhsT=win_sb[:, 128 * j:128 * j + 128],
                    rhs=xf[sl][:, 512 * j:512 * j + 512], start=(j == 0), stop=(j == 7)),
                    chain=(j != 0), reads=[R_xf[sl][j], R_win], writes=[R_pb[b0]])
            S.emit("scalar", lambda e, sl=sl, b0=b0: e.activation(
                out=ssbf[sl][:, :], in_=pb[b0][0:64, :], func=AF.Identity, bias=binb[:, :]),
                reads=[R_pb[b0], R_binb], writes=[R_ssbf[sl]])
            if pump_fn:
                pump_fn()
            for j in range(8):
                b = banks[j % 2]
                S.emit("tensor", lambda e, j=j, b=b, sl=sl: e.matmul(
                    pb[b][:, :], lhsT=bd1_sb[:, 128 * j:128 * j + 128],
                    rhs=ssbf[sl][:, :], start=True, stop=True),
                    reads=[R_ssbf[sl], R_bd1], writes=[R_pb[b]])
                if j % 2 == 0:
                    S.emit("scalar", lambda e, j=j, b=b, hsb=hsb: e.activation(
                        out=hsb[:, 512 * j:512 * j + 512], in_=pb[b][:, :],
                        func=AF.Relu, bias=b1f[:, j:j + 1]),
                        reads=[R_pb[b], R_b1f], writes=[R_h[j]])
                else:
                    S.emit("vector", lambda e, j=j, b=b, hsb=hsb: e.tensor_scalar(
                        out=hsb[:, 512 * j:512 * j + 512], in0=pb[b][:, :],
                        scalar1=b1f[:, j:j + 1], scalar2=0.0,
                        op0=ALU.add, op1=ALU.max),
                        reads=[R_pb[b], R_b1f], writes=[R_h[j]])
            if part == 0:
                return
            _ffn_part2(sl, banks, hsb, R_h, b0, pump_fn)

        def _ffn_part2(sl, banks, hsb, R_h, b0, pump_fn=None):
            for j in range(8):
                S.emit("tensor", lambda e, j=j, sl=sl, b0=b0, hsb=hsb: e.matmul(
                    pb[b0][:, :], lhsT=bd2_sb[:, 128 * j:128 * j + 128],
                    rhs=hsb[:, 512 * j:512 * j + 512], start=(j == 0), stop=(j == 7)),
                    chain=(j != 0), reads=[R_h[j], R_bd2], writes=[R_pb[b0]])
            S.emit("scalar", lambda e, sl=sl, b0=b0: e.activation(
                out=gsb[sl][0:64, :], in_=pb[b0][0:64, :], func=AF.Identity, bias=b2b[:, :]),
                reads=[R_pb[b0], R_b2b], writes=[R_gsb[sl]])
            if pump_fn:
                pump_fn()
            for g in range(NG):
                for f in range(2):
                    b = banks[(2 * g + f) % 2]
                    S.emit("tensor", lambda e, g=g, f=f, b=b, sl=sl: e.matmul(
                        pb[b][:, :], lhsT=gsb[sl][:, 128 * g:128 * g + 128],
                        rhs=woa_sb[:, 512 * f:512 * f + 512], start=True, stop=True),
                        reads=[R_gsb[sl], R_woa], writes=[R_pb[b]])
                    S.emit("vector", lambda e, g=g, f=f, b=b, sl=sl: e.scalar_tensor_tensor(
                        out=X[sl][:, 1024 * g + 512 * f:1024 * g + 512 * f + 512],
                        in0=pb[b][:, :], scalar=1.0,
                        in1=X[sl][:, 1024 * g + 512 * f:1024 * g + 512 * f + 512],
                        op0=ALU.mult, op1=ALU.add,
                        accum_out=xsums[:, 8 * sl + 2 * g + f:8 * sl + 2 * g + f + 1]),
                        reads=[R_pb[b], R_X[sl][g]], writes=[R_X[sl][g], R_xsums[sl]])

        def qkv_qk2(sl, banks, grp):
            for k16 in range(2 * grp, 2 * grp + 2):
                isq = k16 < 8
                j = k16 % 8
                b = banks[k16 % 2]
                dst = qsb[sl] if isq else ksb[sl]
                rdst = R_q[sl][j] if isq else R_k[sl][j]
                for c in range(8):
                    S.emit("tensor", lambda e, k16=k16, b=b, c=c, sl=sl: e.matmul(
                        pb[b][:, :],
                        lhsT=wa_sb[:, 3072 * c + 128 * k16:3072 * c + 128 * k16 + 128],
                        rhs=xf[sl][:, 512 * c:512 * c + 512], start=(c == 0), stop=(c == 7)),
                        chain=(c != 0), reads=[R_xf[sl][c], R_wa], writes=[R_pb[b]])
                S.emit("scalar", lambda e, b=b: e.activation(out=etmp, in_=pb[b][:, :], func=AF.Exp),
                       reads=[R_pb[b]], writes=[R_etmp])
                S.emit("scalar", lambda e, b=b: e.activation(out=rtmp, in_=pb[b][:, :], func=AF.Relu),
                       reads=[R_pb[b]], writes=[R_rtmp])
                S.emit("vector", lambda e, j=j, dst=dst: e.scalar_tensor_tensor(
                    out=dst[:, 512 * j:512 * j + 512], in0=etmp, scalar=1.0,
                    in1=rtmp, op0=ALU.min, op1=ALU.add),
                    reads=[R_etmp, R_rtmp], writes=[rdst])

        def qkv_v(sl, banks, vg):
            for g in (vg,):
                for f in range(2):
                    b = banks[(2 * g + f) % 2]
                    for c in range(8):
                        S.emit("tensor", lambda e, g=g, f=f, b=b, c=c, sl=sl: e.matmul(
                            pb[b][:, :],
                            lhsT=xf[sl][:, 512 * c + 128 * g:512 * c + 128 * g + 128],
                            rhs=wa_sb[:, 3072 * c + 2048 + 512 * f:3072 * c + 2048 + 512 * f + 512],
                            start=(c == 0), stop=(c == 7)),
                            chain=(c != 0), reads=[R_xf[sl][c], R_wa], writes=[R_pb[b]])
                    dstap = vsb[sl][:, 1056 * g + 528 * f:1056 * g + 528 * f + 528] \
                        .rearrange("p (h c) -> p h c", c=66)[:, :, 0:64]
                    S.emit("vector", lambda e, b=b, dstap=dstap: e.tensor_copy(
                        out=dstap, in_=pb[b][:, :].rearrange("p (h c) -> p h c", c=64)),
                        reads=[R_pb[b]], writes=[R_v[sl][g]])

        def scan_p1(sl, t, c):
            for j in range(8):
                b = j // 4
                S.emit("tensor", lambda e, j=j, c=c, b=b, sl=sl: e.matmul(
                    pb[b][:, 128 * (j % 4):128 * (j % 4) + 128],
                    lhsT=ksb[sl][:, 512 * j + 128 * c:512 * j + 128 * c + 128],
                    rhs=ident[:], start=(j % 4 == 0), stop=(j % 4 == 3), skip_group_check=True),
                    chain=(j % 4 != 0), reads=[R_k[sl][j], R_id], writes=[R_pb[b]])
            for b in range(2):
                S.emit("vector", lambda e, b=b: e.tensor_copy(
                    out=ktmc[:, 512 * b:512 * b + 512], in_=pb[b][:, :]),
                    reads=[R_pb[b]], writes=[R_ktmc[b]])
            mb_ = mask4[:, :].unsqueeze(1).broadcast_to([128, 4, 128])
            for a2 in range(2):
                bs = 2 - 2 * a2
                for hh in (0, 2, 4, 6, 1, 3, 5, 7):
                    h = 8 * a2 + hh
                    b = bs + hh % 2
                    po = 64 * (h % 2)
                    j = h // 2
                    S.emit("tensor", lambda e, h=h, b=b, po=po, j=j, c=c, hh=hh, sl=sl: e.matmul(
                        pb[b][:, 128 * (hh // 2):128 * (hh // 2) + 128],
                        lhsT=ksb[sl][po:po + 64, 512 * j + 128 * c:512 * j + 128 * c + 128],
                        rhs=qsb[sl][po:po + 64, 512 * j + 128 * c:512 * j + 128 * c + 128],
                        start=(hh in (0, 1)), stop=(hh in (6, 7)), skip_group_check=True),
                        chain=(hh not in (0, 1)), reads=[R_k[sl][j], R_q[sl][j]], writes=[R_pb[b]])
                for bb in range(2):
                    blk = 2 * a2 + bb
                    S.emit("vector", lambda e, bb=bb, blk=blk, bs=bs, mb_=mb_: e.tensor_tensor(
                        out=atsb[:, 512 * blk:512 * blk + 512].rearrange("p (h c) -> p h c", c=128),
                        in0=pb[bs + bb][:, :].rearrange("p (h c) -> p h c", c=128),
                        in1=mb_, op=ALU.mult),
                        reads=[R_pb[bs + bb], R_mask], writes=[R_at[blk]])

        def scan_p2(sl, t, c):
            gc = (t % 4) * 4 + c
            first = gc == 0
            for a2 in range(2):
                bs = 2 - 2 * a2
                for hh in range(8):
                    h = 8 * a2 + hh
                    b = bs + hh // 4
                    col = 65 * (hh % 4)
                    po = 64 * (h % 2)
                    j = h // 2
                    if not first:
                        S.emit("tensor", lambda e, h=h, b=b, col=col, po=po, j=j, c=c, hh=hh, sl=sl: e.matmul(
                            pb[b][:, col:col + 65],
                            lhsT=qsb[sl][po:po + 64, 512 * j + 128 * c:512 * j + 128 * c + 128],
                            rhs=Ssb[po:po + 64, 65 * (h // 2):65 * (h // 2) + 65],
                            start=(hh % 4 == 0), stop=False, skip_group_check=True),
                            chain=(hh % 4 != 0), reads=[R_q[sl][j], R_Ssb[h // 8]], writes=[R_pb[b]])
                    ao = 1024 * (h // 8) + 512 * (h % 2) + 128 * ((h % 8) // 2)
                    S.emit("tensor", lambda e, h=h, b=b, col=col, ao=ao, c=c, hh=hh, first=first, sl=sl: e.matmul(
                        pb[b][:, col:col + 65],
                        lhsT=atsb[:, ao:ao + 128],
                        rhs=vsb[sl][:, 1056 * c + 66 * h:1056 * c + 66 * h + 65],
                        start=(first and hh % 4 == 0), stop=True, skip_group_check=True),
                        chain=(not (first and hh % 4 == 0)), reads=[R_at[2 * (h // 8) + h % 2], R_v[sl][c]], writes=[R_pb[b]])
                for bb in range(2):
                    base = 8 * a2 + 4 * bb
                    src = pb[bs + bb][:, 0:260].rearrange("p (h c) -> p h c", c=65)[:, :, 64]
                    S.emit("vector", lambda e, base=base, src=src, bs=bs, bb=bb: e.tensor_scalar(
                        out=dgat[:, base:base + 4], in0=src, scalar1=EPS_AT, scalar2=None, op0=ALU.add),
                        reads=[R_pb[bs + bb]], writes=[R_dgat])
                S.emit("vector", lambda e, a2=a2: e.reciprocal(
                    out=rcp[:, 8 * a2:8 * a2 + 8], in_=dgat[:, 8 * a2:8 * a2 + 8]),
                    reads=[R_dgat], writes=[R_rcp])
                for bb in range(2):
                    base = 8 * a2 + 4 * bb
                    src = pb[bs + bb][:, 0:260].rearrange("p (h c) -> p h c", c=65)[:, :, 0:64]
                    rb = rcp[:, base:base + 4].unsqueeze(2).broadcast_to([128, 4, 64])
                    dst = ytm[:, 64 * base:64 * base + 256].rearrange("p (h c) -> p h c", c=64)
                    S.emit("vector", lambda e, src=src, rb=rb, dst=dst, bs=bs, bb=bb: e.tensor_tensor(
                        out=dst, in0=src, in1=rb, op=ALU.mult),
                        reads=[R_pb[bs + bb], R_rcp], writes=[R_ytm[a2]])

        def scan_p3(sl, t, c):
            gc = (t % 4) * 4 + c
            first = gc == 0
            if first:
                for b in (5, 6):
                    S.emit("tensor", lambda e, b=b: e.matmul(
                        pb[b][:, 0:260], lhsT=ident[:], rhs=zsb,
                        start=True, stop=False, skip_group_check=True),
                        reads=[R_id, R_zsb], writes=[R_pb[b]])
            for h in range(16):
                p2 = h // 2
                b = 5 + p2 // 4
                col = 65 * (p2 % 4)
                po = 64 * (h % 2)
                S.emit("tensor", lambda e, h=h, b=b, col=col, po=po, c=c, sl=sl: e.matmul(
                    pb[b][po:po + 64, col:col + 65],
                    lhsT=ktmc[:, 64 * h:64 * h + 64],
                    rhs=vsb[sl][:, 1056 * c + 66 * h:1056 * c + 66 * h + 65],
                    start=False, stop=True, skip_group_check=True,
                    tile_position=(0, po)),
                    chain=True, reads=[R_ktmc[h // 8], R_v[sl][c]], writes=[R_pb[b]])
            S.emit("vector", lambda e: e.tensor_copy(out=Ssb[:, 0:260], in_=pb[5][:, 0:260]),
                   reads=[R_pb[5]], writes=[R_Ssb[0]])
            S.emit("vector", lambda e: e.tensor_copy(out=Ssb[:, 260:520], in_=pb[6][:, 0:260]),
                   reads=[R_pb[6]], writes=[R_Ssb[1]])
            for j in range(8):
                b = 2 + j // 4
                S.emit("tensor", lambda e, j=j, b=b: e.matmul(
                    pb[b][:, 128 * (j % 4):128 * (j % 4) + 128],
                    lhsT=ytm[:, 128 * j:128 * j + 128],
                    rhs=ident[:], start=(j % 4 == 0), stop=(j % 4 == 3), skip_group_check=True),
                    chain=(j % 4 != 0), reads=[R_ytm[j // 4], R_id], writes=[R_pb[b]])
            for b in range(2):
                S.emit("vector", lambda e, b=b: e.tensor_copy(
                    out=yfc[:, 512 * b:512 * b + 512], in_=pb[2 + b][:, :]),
                    reads=[R_pb[2 + b]], writes=[R_yfc[b]])
            for f in range(2):
                b = f
                for j in range(8):
                    S.emit("tensor", lambda e, f=f, b=b, j=j: e.matmul(
                        pb[b][:, :],
                        lhsT=yfc[:, 128 * j:128 * j + 128],
                        rhs=wp_sb[:, 1024 * j + 512 * f:1024 * j + 512 * f + 512],
                        start=(j == 0), stop=(j == 7)),
                        chain=(j != 0), reads=[R_yfc[j // 4], R_wp], writes=[R_pb[b]])
                S.emit("vector", lambda e, f=f, b=b, c=c, sl=sl: e.scalar_tensor_tensor(
                    out=X[sl][:, 1024 * c + 512 * f:1024 * c + 512 * f + 512],
                    in0=pb[b][:, :], scalar=1.0,
                    in1=X[sl][:, 1024 * c + 512 * f:1024 * c + 512 * f + 512],
                    op0=ALU.mult, op1=ALU.add,
                    accum_out=xsums[:, 8 * sl + 2 * c + f:8 * sl + 2 * c + f + 1]),
                    reads=[R_pb[b], R_X[sl][c]], writes=[R_X[sl][c], R_xsums[sl]])

        FRONT_BANKS = (4, 7)
        BACK_BANKS23 = (2, 3)

        def transpose_xh_half(sl, banks, hlf):
            for g in (2 * hlf, 2 * hlf + 1):
                for j in range(8):
                    b = banks[j // 4]
                    S.emit("tensor", lambda e, j=j, g=g, b=b, sl=sl: e.matmul(
                        pb[b][:, 128 * (j % 4):128 * (j % 4) + 128],
                        lhsT=xh[sl][:, 1024 * g + 128 * j:1024 * g + 128 * j + 128],
                        rhs=ident[:], start=(j % 4 == 0), stop=(j % 4 == 3), skip_group_check=True),
                        chain=(j % 4 != 0), reads=[R_xh[sl][g], R_id], writes=[R_pb[b]])
                for bb in range(2):
                    dst = xf[sl][:, 2048 * bb:2048 * bb + 2048].rearrange(
                        "p (j t) -> p j t", t=512)[:, :, 128 * g:128 * g + 128]
                    S.emit("scalar", lambda e, g=g, bb=bb, sl=sl, dst=dst: e.activation(
                        out=dst, in_=pb[banks[bb]][:, :].rearrange("p (j t) -> p j t", t=128),
                        func=AF.Identity),
                        reads=[R_pb[banks[bb]]],
                        writes=[R_xf[sl][4 * bb], R_xf[sl][4 * bb + 1], R_xf[sl][4 * bb + 2], R_xf[sl][4 * bb + 3]])

        def front_sections(t):
            sl = t % 2
            seq, tl = t // 4, t % 4
            secs = []
            src = x_d[seq, 512 * tl:512 * tl + 512, :].rearrange("(g p) c -> p g c", p=128)
            secs.append(lambda src=src, sl=sl: S.emit("sync", lambda e, src=src, sl=sl: e.dma_start(
                out=X[sl][:, :].rearrange("p (g c) -> p g c", c=1024), in_=src),
                writes=list(R_X[sl]), dma=True))
            secs.append(lambda sl=sl: ln_pass(sl, FRONT_BANKS))
            secs.append(lambda sl=sl: transpose_xh_half(sl, FRONT_BANKS, 0))
            secs.append(lambda sl=sl: transpose_xh_half(sl, FRONT_BANKS, 1))
            secs.append(lambda sl=sl: ffn(sl, FRONT_BANKS, part=0))
            secs.append(lambda sl=sl: ffn(sl, FRONT_BANKS, part=1))
            secs.append(lambda sl=sl: ln_pass(sl, FRONT_BANKS, folded=True))
            secs.append(lambda sl=sl: transpose_xh_half(sl, FRONT_BANKS, 0))
            secs.append(lambda sl=sl: transpose_xh_half(sl, FRONT_BANKS, 1))
            for grp in range(8):
                secs.append(lambda sl=sl, grp=grp: qkv_qk2(sl, FRONT_BANKS, grp))
            for vg in range(4):
                secs.append(lambda sl=sl, vg=vg: qkv_v(sl, FRONT_BANKS, vg))
            secs.append(None)
            return secs

        from collections import deque
        fq = deque()

        def pump(n=1):
            for _ in range(n):
                if fq:
                    s = fq.popleft()
                    if s is not None:
                        s()

        def back(t):
            sl = t % 2
            seq, tl = t // 4, t % 4
            for c in range(4):
                scan_p1(sl, t, c)
                pump()
                scan_p2(sl, t, c)
                pump()
                scan_p3(sl, t, c)
                pump()
            ln_pass(sl, BACK_BANKS23, folded=True)
            pump()
            transpose_xh_half(sl, BACK_BANKS23, 0)
            pump()
            transpose_xh_half(sl, BACK_BANKS23, 1)
            pump()
            ffn(sl, BACK_BANKS23, pump_fn=pump)
            pump()
            dst = out_d[seq, 512 * tl:512 * tl + 512, :].rearrange("(g p) c -> p g c", p=128)
            S.emit("gpsimd", lambda e, dst=dst, sl=sl: e.dma_start(
                out=dst, in_=X[sl][:, :].rearrange("p (g c) -> p g c", c=1024)),
                reads=list(R_X[sl]), dma=True, force_inc=True)

        fq.extend(front_sections(0))
        while fq:
            s = fq.popleft()
            if s is not None:
                s()
        for t in range(NTILE):
            if t + 1 < NTILE:
                fq.extend(front_sections(t + 1))
            back(t)
            while fq:
                s = fq.popleft()
                if s is not None:
                    s()

        run_engine = S.replay(sems)

        @block.sync
        def _(sync):
            run_engine(sync, "sync")

        @block.tensor
        def _(tensor):
            run_engine(tensor, "tensor")

        @block.vector
        def _(vector):
            run_engine(vector, "vector")

        @block.scalar
        def _(scalar):
            run_engine(scalar, "scalar")

        @block.gpsimd
        def _(gpsimd):
            run_engine(gpsimd, "gpsimd")
            gpsimd.wait_ge(sems["gpsimd"], S.final_cnt["gpsimd"])

    return nc


_NC = None


def _get_nc():
    global _NC
    if _NC is None:
        _NC = build_program()
    return _NC


def kernel(L0_attn_x, L0_attn_w_attn, L0_attn_w_proj, L0_ffn_Win, L0_ffn_bin_,
           L0_ffn_W1, L0_ffn_b1, L0_ffn_W2, L0_ffn_b2, L0_ffn_Wout, L0_ffn_bout):
    x = np.ascontiguousarray(np.asarray(L0_attn_x, dtype=np.float32))
    wa = np.ascontiguousarray(np.asarray(L0_attn_w_attn, dtype=np.float32))
    wp = np.ascontiguousarray(np.asarray(L0_attn_w_proj, dtype=np.float32))
    win = np.asarray(L0_ffn_Win, dtype=np.float32)
    binv = np.asarray(L0_ffn_bin_, dtype=np.float32)
    W1 = np.asarray(L0_ffn_W1, dtype=np.float32)
    b1 = np.asarray(L0_ffn_b1, dtype=np.float32)
    W2 = np.asarray(L0_ffn_W2, dtype=np.float32)
    b2 = np.asarray(L0_ffn_b2, dtype=np.float32)
    wout = np.ascontiguousarray(np.asarray(L0_ffn_Wout, dtype=np.float32))
    bout = np.asarray(L0_ffn_bout, dtype=np.float32)

    win_r = np.ascontiguousarray(
        np.concatenate([win.reshape(8, 128, 64)] * 2, axis=2).transpose(1, 0, 2).reshape(128, 1024))
    bd1 = np.zeros((64, 1024), np.float32)
    m_idx = np.arange(64)
    for i in range(16):
        bd1[m_idx, 16 * m_idx + i] = W1[:, i]
    bd2 = np.zeros((1024, 64), np.float32)
    for i in range(16):
        bd2[16 * m_idx + i, m_idx] = W2[:, i]
    bd2_r = np.ascontiguousarray(
        np.concatenate([bd2.reshape(8, 128, 64)] * 2, axis=2).transpose(1, 0, 2).reshape(128, 1024))
    b1f = np.ascontiguousarray(b1.reshape(1024).reshape(8, 128).T)
    binb = np.ascontiguousarray(binv.reshape(64, 1))
    b2b = np.ascontiguousarray(b2.reshape(64, 1))
    bout_r = np.ascontiguousarray(bout.reshape(1, 1024))
    ident = np.eye(128, dtype=ml_dtypes.bfloat16)
    mask4 = np.ascontiguousarray(np.triu(np.ones((128, 128), np.float32)).astype(ml_dtypes.bfloat16))

    shared = dict(w_attn=wa, w_proj=wp, win_r=win_r, bd1=bd1, bd2_r=bd2_r,
                  wout=wout, bout_r=bout_r, binb=binb, b1f=b1f, b2b=b2b,
                  ident=ident, mask4=mask4)
    in_maps = []
    for i in range(NCORE):
        m = dict(shared)
        m["x"] = np.ascontiguousarray(x[2 * i:2 * i + 2])
        in_maps.append(m)

    nc = _get_nc()
    res = run_bass_kernel_spmd(nc, in_maps, core_ids=list(range(NCORE)))
    out = np.concatenate([np.asarray(res.results[i]["out"]) for i in range(NCORE)], axis=0)
    return out


# version 54
# speedup vs baseline: 1.0297x; 1.0236x over previous
import numpy as np
import ml_dtypes
import concourse.bass as bass
import concourse.mybir as mybir
from concourse.bass_utils import run_bass_kernel_spmd

F32 = mybir.dt.float32
BF16 = mybir.dt.bfloat16
AF = mybir.ActivationFunctionType
ALU = mybir.AluOpType

B, T, C = 16, 2048, 1024
NCORE = 8
H, D = 16, 64
TILE = 512
NTILE = 8
NG = 4
EPS_LN = 1e-5
EPS_AT = 1e-6
INV_C = 1.0 / 1024.0

ENGS = ("sync", "tensor", "vector", "scalar", "gpsimd")


class Res:
    __slots__ = ("w", "r")

    def __init__(self):
        self.w = []
        self.r = []


class Sched:

    def __init__(self):
        self.ops = []
        self.forced = set()
        self.chained = set()

    def emit(self, eng, fn, reads=(), writes=(), deps=(), dma=False, force_inc=False, chain=False):
        deps = set(deps)
        for r in reads:
            deps.update(r.w)
        for r in writes:
            deps.update(r.w)
            deps.update(r.r)
        i = len(self.ops)
        self.ops.append((eng, fn, sorted(deps), 16 if dma else 1))
        if chain:
            self.chained.add(i)
        if force_inc or dma:
            self.forced.add(i)
        for r in reads:
            r.r.append(i)
        for r in writes:
            r.w = [i]
            r.r = []
        return i

    def replay(self, sems):
        n = len(self.ops)
        need = [False] * n
        for i in self.forced:
            need[i] = True
        self_sync = {}
        prev_own = {}
        for i, (eng, fn, deps, amt) in enumerate(self.ops):
            p = prev_own.get(eng)
            if p is not None and p in deps and i not in self.chained:
                self_sync[i] = p
                need[p] = True
            prev_own[eng] = i
        for eng, fn, deps, amt in self.ops:
            for d in deps:
                if self.ops[d][0] != eng:
                    need[d] = True
        val = [0] * n
        cnt = {e: 0 for e in ENGS}
        for i, (eng, fn, deps, amt) in enumerate(self.ops):
            if need[i]:
                cnt[eng] += amt
                val[i] = cnt[eng]
        self.final_cnt = cnt

        def run_engine(engine_obj, ename):
            waited = {e: 0 for e in ENGS}
            own_dma = 0
            for i, (eng, fn, deps, amt) in enumerate(self.ops):
                if eng != ename:
                    continue
                if amt == 16:
                    if own_dma > 0 and waited[ename] < 16 * own_dma:
                        engine_obj.wait_ge(sems[ename], 16 * own_dma)
                        waited[ename] = 16 * own_dma
                    own_dma += 1
                want = {}
                for d in deps:
                    de = self.ops[d][0]
                    if de == ename:
                        continue
                    v = val[d]
                    assert v > 0, f"dep {d} of op {i} has no inc"
                    if v > want.get(de, 0):
                        want[de] = v
                if i in self_sync:
                    v = val[self_sync[i]]
                    if v > want.get(ename, 0):
                        want[ename] = v
                for de, v in want.items():
                    if waited[de] < v:
                        engine_obj.wait_ge(sems[de], v)
                        waited[de] = v
                ins = fn(engine_obj)
                if need[i]:
                    ins.then_inc(sems[eng], amt)
        return run_engine


def build_program():
    nc = bass.Bass("TRN2", target_bir_lowering=False)

    x_d = nc.dram_tensor("x", [2, T, C], F32, kind="ExternalInput").ap()
    out_d = nc.dram_tensor("out", [2, T, C], F32, kind="ExternalOutput").ap()
    wa_d = nc.dram_tensor("w_attn", [C, 3 * C], F32, kind="ExternalInput").ap()
    wp_d = nc.dram_tensor("w_proj", [C, C], F32, kind="ExternalInput").ap()
    win_d = nc.dram_tensor("win_r", [128, 1024], F32, kind="ExternalInput").ap()
    bd1_d = nc.dram_tensor("bd1", [64, 1024], F32, kind="ExternalInput").ap()
    bd2_d = nc.dram_tensor("bd2_r", [128, 1024], F32, kind="ExternalInput").ap()
    wout_d = nc.dram_tensor("wout", [64, 1024], F32, kind="ExternalInput").ap()
    bout_d = nc.dram_tensor("bout_r", [1, 1024], F32, kind="ExternalInput").ap()
    binb_d = nc.dram_tensor("binb", [64, 1], F32, kind="ExternalInput").ap()
    b1f_d = nc.dram_tensor("b1f", [128, 8], F32, kind="ExternalInput").ap()
    b2b_d = nc.dram_tensor("b2b", [64, 1], F32, kind="ExternalInput").ap()
    id_d = nc.dram_tensor("ident", [128, 128], BF16, kind="ExternalInput").ap()
    mask_d = nc.dram_tensor("mask4", [128, 128], BF16, kind="ExternalInput").ap()

    from contextlib import ExitStack
    with ExitStack() as _ctx:
        def sb(name, shape, dt):
            return _ctx.enter_context(nc.sbuf_tensor(name, shape, dt))
        wa_sb = sb("wa_sb", [128, 8 * 3072], BF16)
        wp_sb = sb("wp_sb", [128, 8 * 1024], BF16)
        win_sb = sb("win_sb", [128, 1024], BF16)
        bd1_sb = sb("bd1_sb", [64, 1024], BF16)
        bd2_sb = sb("bd2_sb", [128, 1024], BF16)
        woa_sb = sb("woa_sb", [65, 1024], BF16)
        ident = sb("ident_sb", [128, 128], BF16)
        mask4 = sb("mask4_sb", [128, 128], BF16)
        binb = sb("binb_sb", [64, 1], F32)
        b1f = sb("b1f_sb", [128, 8], F32)
        b2b = sb("b2b_sb", [64, 1], F32)
        stgA = sb("stgA", [128, 1536], F32)
        stgB = sb("stgB", [128, 1536], F32)
        X = [sb("X0", [128, 4096], F32), sb("X1", [128, 4096], F32)]
        xh = [sb("xh0", [128, 4096], BF16), sb("xh1", [128, 4096], BF16)]
        xf = [sb("xf0", [128, 4096], BF16), sb("xf1", [128, 4096], BF16)]
        qsb = [sb("qsb0", [128, 4096], BF16), sb("qsb1", [128, 4096], BF16)]
        ksb = [sb("ksb0", [128, 4096], BF16), sb("ksb1", [128, 4096], BF16)]
        vsb = [sb("vsb0", [128, 4224], BF16), sb("vsb1", [128, 4224], BF16)]
        ssbf = [sb("ssbf0", [64, 512], BF16), sb("ssbf1", [64, 512], BF16)]
        gsb = [sb("gsb0", [65, 512], BF16), sb("gsb1", [65, 512], BF16)]
        stats = sb("stats", [128, 64], F32)
        xsums = sb("xsums", [128, 16], F32)
        epsb = sb("epsb", [128, 1], F32)
        atsb = sb("atsb", [128, 2048], BF16)
        Ssb = sb("Ssb", [128, 520], BF16)
        dgat = sb("dgat", [128, 16], F32)
        rcp = sb("rcp", [128, 16], F32)
        stgA_bf = stgA[:, :].bitcast(BF16)
        stgB_bf = stgB[:, :].bitcast(BF16)
        etmp = stgA_bf[:, 0:512]
        rtmp = stgA_bf[:, 512:1024]
        yfc = stgA_bf[:, 1024:2048]
        ktmc = stgB_bf[:, 0:1024]
        zsb = stgB_bf[:, 1024:1284]
        ytm = stgB_bf[:, 1284:2308]
        pb = [_ctx.enter_context(nc.psum_tensor(f"pb{i}", [128, 512], F32)) for i in range(8)]
        sems = {e: _ctx.enter_context(nc.semaphore(f"sem_{e}")) for e in ENGS}
        block = _ctx.enter_context(nc.Block())

        S = Sched()
        R_pb = [Res() for _ in range(8)]
        R_stgA, R_stgB = Res(), Res()
        R_wa, R_wp, R_win, R_bd1, R_bd2, R_woa = (Res() for _ in range(6))
        R_id, R_mask, R_binb, R_b1f, R_b2b, R_zsb = (Res() for _ in range(6))
        R_X = [[Res() for _ in range(NG)] for _ in range(2)]
        R_xh = [[Res() for _ in range(NG)] for _ in range(2)]
        R_xf = [[Res() for _ in range(8)] for _ in range(2)]
        R_q = [[Res() for _ in range(8)] for _ in range(2)]
        R_k = [[Res() for _ in range(8)] for _ in range(2)]
        R_v = [[Res() for _ in range(NG)] for _ in range(2)]
        R_ssbf = [Res(), Res()]
        R_gsb = [Res(), Res()]
        R_ktmc = [Res(), Res()]
        R_at = [Res() for _ in range(4)]
        R_Ssb = [Res(), Res()]
        R_ytm = [Res(), Res()]
        R_yfc = [Res(), Res()]
        R_etmp, R_rtmp = Res(), Res()
        R_dumpv, R_dumps = Res(), Res()
        R_sums = [Res(), Res()]
        R_sumsq = [Res(), Res()]
        R_mu = [Res(), Res()]
        R_musq = [Res(), Res()]
        R_varr = [Res(), Res()]
        R_stdv = [Res(), Res()]
        R_rr = [Res(), Res()]
        R_nmur = [Res(), Res()]
        R_dgat, R_rcp = Res(), Res()
        R_xsums = [Res(), Res()]
        R_epsb = Res()

        S.emit("sync", lambda e: e.dma_start(out=ident[:], in_=id_d), writes=[R_id], dma=True)
        S.emit("sync", lambda e: e.dma_start(out=mask4[:], in_=mask_d), writes=[R_mask], dma=True)
        S.emit("sync", lambda e: e.dma_start(out=binb[:], in_=binb_d), writes=[R_binb], dma=True)
        S.emit("sync", lambda e: e.dma_start(out=b1f[:], in_=b1f_d), writes=[R_b1f], dma=True)
        S.emit("sync", lambda e: e.dma_start(out=b2b[:], in_=b2b_d), writes=[R_b2b], dma=True)

        S.emit("vector", lambda e: e.memset(epsb[:, :], EPS_LN), writes=[R_epsb])
        for sl in range(2):
            S.emit("vector", lambda e, sl=sl: e.memset(gsb[sl][64:65, :], 1.0), writes=[R_gsb[sl]])
            for g in range(NG):
                ap = vsb[sl][:, 1056 * g:1056 * g + 1056].rearrange("p (h c) -> p h c", c=66)[:, :, 64]
                S.emit("vector", lambda e, ap=ap: e.memset(ap, 1.0), writes=[R_v[sl][g]])

        _stg_flip = [0]

        def staged(src_ap, rows, cols, dst_ap, wres, scale=None):
            stg, rstg = (stgA, R_stgA) if _stg_flip[0] == 0 else (stgB, R_stgB)
            q = "gpsimd" if _stg_flip[0] == 0 else "sync"
            _stg_flip[0] ^= 1
            sl = stg[0:rows, 0:cols]
            S.emit(q, lambda e, s=src_ap, d=sl: e.dma_start(out=d, in_=s),
                   writes=[rstg], dma=True)
            if scale is None:
                S.emit("vector", lambda e, d=sl, o=dst_ap: e.tensor_copy(out=o, in_=d),
                       reads=[rstg], writes=[wres])
            else:
                S.emit("vector", lambda e, d=sl, o=dst_ap: e.tensor_scalar(
                    out=o, in0=d, scalar1=scale, scalar2=None, op0=ALU.mult),
                    reads=[rstg], writes=[wres])

        staged(win_d, 128, 1024, win_sb[:], R_win)
        staged(bd1_d, 64, 1024, bd1_sb[:], R_bd1)
        staged(bd2_d, 128, 1024, bd2_sb[:], R_bd2)
        stg, rstg = (stgA, R_stgA) if _stg_flip[0] == 0 else (stgB, R_stgB)
        _stg_flip[0] ^= 1
        S.emit("gpsimd", lambda e, stg=stg: e.dma_start(out=stg[0:64, 0:1024], in_=wout_d),
               writes=[rstg], dma=True)
        S.emit("gpsimd", lambda e, stg=stg: e.dma_start(out=stg[64:65, 0:1024], in_=bout_d),
               writes=[rstg], dma=True)
        S.emit("vector", lambda e, stg=stg: e.tensor_scalar(
            out=woa_sb[0:65, :], in0=stg[0:65, 0:1024], scalar1=0.5, scalar2=None, op0=ALU.mult),
            reads=[rstg], writes=[R_woa])
        for j in range(8):
            for hlf in range(2):
                staged(wa_d[128 * j:128 * j + 128, 1536 * hlf:1536 * hlf + 1536], 128, 1536,
                       wa_sb[:, 3072 * j + 1536 * hlf:3072 * j + 1536 * hlf + 1536], R_wa)
        for j in range(8):
            staged(wp_d[128 * j:128 * j + 128, :], 128, 1024,
                   wp_sb[:, 1024 * j:1024 * j + 1024], R_wp)
        for r in (R_etmp, R_rtmp, R_yfc[0], R_yfc[1]):
            r.r.extend(R_stgA.r)
            r.w = list(R_stgA.w)
        for r in (R_ktmc[0], R_ktmc[1], R_zsb, R_ytm[0], R_ytm[1]):
            r.r.extend(R_stgB.r)
            r.w = list(R_stgB.w)
        S.emit("vector", lambda e: e.memset(zsb, 0.0), writes=[R_zsb])

        def ln_pass(sl, fb, folded=False):
            o = 32 * sl
            sums = stats[:, o + 0:o + 4]
            sumsq = stats[:, o + 4:o + 8]
            mu = stats[:, o + 8:o + 12]
            musq = stats[:, o + 12:o + 16]
            varr = stats[:, o + 16:o + 20]
            stdv = stats[:, o + 20:o + 24]
            rr = stats[:, o + 24:o + 28]
            nmur = stats[:, o + 28:o + 32]
            for g in range(NG):
                if not folded:
                    S.emit("vector", lambda e, g=g, sl=sl, sums=sums: e.tensor_scalar(
                        out=xh[sl][:, 0:1024], in0=X[sl][:, 1024 * g:1024 * g + 1024],
                        scalar1=1.0, scalar2=0.0, op0=ALU.mult, op1=ALU.add,
                        accum_out=sums[:, g:g + 1]),
                        reads=[R_X[sl][g]], writes=[R_xh[sl][0], R_sums[sl]])
                S.emit("scalar", lambda e, g=g, sl=sl, sumsq=sumsq: e.activation(
                    out=xh[sl][:, 1024:2048], in_=X[sl][:, 1024 * g:1024 * g + 1024],
                    func=AF.Square, accum_out=sumsq[:, g:g + 1]),
                    reads=[R_X[sl][g]], writes=[R_xh[sl][1], R_sumsq[sl]])
            if folded:
                S.emit("vector", lambda e, sl=sl, sums=sums: e.tensor_tensor(
                    out=sums,
                    in0=xsums[:, 8 * sl:8 * sl + 8].rearrange("p (g f) -> p g f", f=2)[:, :, 0],
                    in1=xsums[:, 8 * sl:8 * sl + 8].rearrange("p (g f) -> p g f", f=2)[:, :, 1],
                    op=ALU.add),
                    reads=[R_xsums[sl]], writes=[R_sums[sl]])
            S.emit("vector", lambda e, sums=sums, mu=mu: e.tensor_scalar(
                out=mu, in0=sums, scalar1=INV_C, scalar2=None, op0=ALU.mult),
                reads=[R_sums[sl]], writes=[R_mu[sl]])
            S.emit("vector", lambda e, mu=mu, musq=musq: e.scalar_tensor_tensor(
                out=musq, in0=mu, scalar=1.0, in1=mu, op0=ALU.mult, op1=ALU.mult),
                reads=[R_mu[sl]], writes=[R_musq[sl]])
            S.emit("vector", lambda e, sumsq=sumsq, musq=musq, varr=varr: e.scalar_tensor_tensor(
                out=varr, in0=sumsq, scalar=INV_C, in1=musq, op0=ALU.mult, op1=ALU.subtract),
                reads=[R_sumsq[sl], R_musq[sl]], writes=[R_varr[sl]])
            S.emit("scalar", lambda e, varr=varr, stdv=stdv: e.activation(out=stdv, in_=varr, func=AF.Ln,
                                                                          bias=epsb[:, :]),
                   reads=[R_varr[sl], R_epsb], writes=[R_stdv[sl]])
            S.emit("scalar", lambda e, stdv=stdv, rr=rr: e.activation(out=rr, in_=stdv, func=AF.Exp, scale=-0.5),
                   reads=[R_stdv[sl]], writes=[R_rr[sl]])
            S.emit("vector", lambda e, mu=mu, rr=rr, nmur=nmur: e.scalar_tensor_tensor(
                out=nmur, in0=mu, scalar=-1.0, in1=rr, op0=ALU.mult, op1=ALU.mult),
                reads=[R_mu[sl], R_rr[sl]], writes=[R_nmur[sl]])
            for g in range(NG):
                if g % 2 == 0:
                    S.emit("scalar", lambda e, g=g, sl=sl, rr=rr, nmur=nmur: e.activation(
                        out=xh[sl][:, 1024 * g:1024 * g + 1024], in_=X[sl][:, 1024 * g:1024 * g + 1024],
                        func=AF.Identity, scale=rr[:, g:g + 1], bias=nmur[:, g:g + 1]),
                        reads=[R_X[sl][g], R_rr[sl], R_nmur[sl]], writes=[R_xh[sl][g]])
                else:
                    S.emit("vector", lambda e, g=g, sl=sl, rr=rr, nmur=nmur: e.tensor_scalar(
                        out=xh[sl][:, 1024 * g:1024 * g + 1024], in0=X[sl][:, 1024 * g:1024 * g + 1024],
                        scalar1=rr[:, g:g + 1], scalar2=nmur[:, g:g + 1], op0=ALU.mult, op1=ALU.add),
                        reads=[R_X[sl][g], R_rr[sl], R_nmur[sl]], writes=[R_xh[sl][g]])

        def transpose_xh(sl, banks):
            for j in range(8):
                b = banks[j % 2]
                for g in range(NG):
                    S.emit("tensor", lambda e, j=j, g=g, b=b, sl=sl: e.matmul(
                        pb[b][:, 128 * g:128 * g + 128],
                        lhsT=xh[sl][:, 1024 * g + 128 * j:1024 * g + 128 * j + 128],
                        rhs=ident[:], start=(g == 0), stop=(g == 3), skip_group_check=True),
                        chain=(g != 0), reads=[R_xh[sl][g], R_id], writes=[R_pb[b]])
                if j % 2 == 0:
                    S.emit("vector", lambda e, j=j, b=b, sl=sl: e.tensor_copy(
                        out=xf[sl][:, 512 * j:512 * j + 512], in_=pb[b][:, :]),
                        reads=[R_pb[b]], writes=[R_xf[sl][j]])
                else:
                    S.emit("scalar", lambda e, j=j, b=b, sl=sl: e.activation(
                        out=xf[sl][:, 512 * j:512 * j + 512], in_=pb[b][:, :], func=AF.Identity),
                        reads=[R_pb[b]], writes=[R_xf[sl][j]])

        def ffn(sl, banks, pump_fn=None, part=None):
            b0, b1 = banks
            hsb, R_h = qsb[sl], R_q[sl]
            if part == 1:
                _ffn_part2(sl, banks, hsb, R_h, b0, pump_fn)
                return
            for j in range(8):
                S.emit("tensor", lambda e, j=j, sl=sl, b0=b0: e.matmul(
                    pb[b0][:, :], lhsT=win_sb[:, 128 * j:128 * j + 128],
                    rhs=xf[sl][:, 512 * j:512 * j + 512], start=(j == 0), stop=(j == 7)),
                    chain=(j != 0), reads=[R_xf[sl][j], R_win], writes=[R_pb[b0]])
            S.emit("scalar", lambda e, sl=sl, b0=b0: e.activation(
                out=ssbf[sl][:, :], in_=pb[b0][0:64, :], func=AF.Identity, bias=binb[:, :]),
                reads=[R_pb[b0], R_binb], writes=[R_ssbf[sl]])
            if pump_fn:
                pump_fn()
            for j in range(8):
                b = banks[j % 2]
                S.emit("tensor", lambda e, j=j, b=b, sl=sl: e.matmul(
                    pb[b][:, :], lhsT=bd1_sb[:, 128 * j:128 * j + 128],
                    rhs=ssbf[sl][:, :], start=True, stop=True),
                    reads=[R_ssbf[sl], R_bd1], writes=[R_pb[b]])
                S.emit("scalar", lambda e, j=j, b=b, hsb=hsb: e.activation(
                    out=hsb[:, 512 * j:512 * j + 512], in_=pb[b][:, :],
                    func=AF.Relu, bias=b1f[:, j:j + 1]),
                    reads=[R_pb[b], R_b1f], writes=[R_h[j]])
            if part == 0:
                return
            _ffn_part2(sl, banks, hsb, R_h, b0, pump_fn)

        def _ffn_part2(sl, banks, hsb, R_h, b0, pump_fn=None):
            for j in range(8):
                S.emit("tensor", lambda e, j=j, sl=sl, b0=b0, hsb=hsb: e.matmul(
                    pb[b0][:, :], lhsT=bd2_sb[:, 128 * j:128 * j + 128],
                    rhs=hsb[:, 512 * j:512 * j + 512], start=(j == 0), stop=(j == 7)),
                    chain=(j != 0), reads=[R_h[j], R_bd2], writes=[R_pb[b0]])
            S.emit("scalar", lambda e, sl=sl, b0=b0: e.activation(
                out=gsb[sl][0:64, :], in_=pb[b0][0:64, :], func=AF.Identity, bias=b2b[:, :]),
                reads=[R_pb[b0], R_b2b], writes=[R_gsb[sl]])
            if pump_fn:
                pump_fn()
            for g in range(NG):
                for f in range(2):
                    b = banks[(2 * g + f) % 2]
                    S.emit("tensor", lambda e, g=g, f=f, b=b, sl=sl: e.matmul(
                        pb[b][:, :], lhsT=gsb[sl][:, 128 * g:128 * g + 128],
                        rhs=woa_sb[:, 512 * f:512 * f + 512], start=True, stop=True),
                        reads=[R_gsb[sl], R_woa], writes=[R_pb[b]])
                    S.emit("vector", lambda e, g=g, f=f, b=b, sl=sl: e.scalar_tensor_tensor(
                        out=X[sl][:, 1024 * g + 512 * f:1024 * g + 512 * f + 512],
                        in0=pb[b][:, :], scalar=1.0,
                        in1=X[sl][:, 1024 * g + 512 * f:1024 * g + 512 * f + 512],
                        op0=ALU.mult, op1=ALU.add,
                        accum_out=xsums[:, 8 * sl + 2 * g + f:8 * sl + 2 * g + f + 1]),
                        reads=[R_pb[b], R_X[sl][g]], writes=[R_X[sl][g], R_xsums[sl]])

        def qkv_qk2(sl, banks, grp):
            for k16 in range(2 * grp, 2 * grp + 2):
                isq = k16 < 8
                j = k16 % 8
                b = banks[k16 % 2]
                dst = qsb[sl] if isq else ksb[sl]
                rdst = R_q[sl][j] if isq else R_k[sl][j]
                for c in range(8):
                    S.emit("tensor", lambda e, k16=k16, b=b, c=c, sl=sl: e.matmul(
                        pb[b][:, :],
                        lhsT=wa_sb[:, 3072 * c + 128 * k16:3072 * c + 128 * k16 + 128],
                        rhs=xf[sl][:, 512 * c:512 * c + 512], start=(c == 0), stop=(c == 7)),
                        chain=(c != 0), reads=[R_xf[sl][c], R_wa], writes=[R_pb[b]])
                S.emit("scalar", lambda e, b=b: e.activation(out=etmp, in_=pb[b][:, :], func=AF.Exp),
                       reads=[R_pb[b]], writes=[R_etmp])
                S.emit("scalar", lambda e, b=b: e.activation(out=rtmp, in_=pb[b][:, :], func=AF.Relu),
                       reads=[R_pb[b]], writes=[R_rtmp])
                S.emit("vector", lambda e, j=j, dst=dst: e.scalar_tensor_tensor(
                    out=dst[:, 512 * j:512 * j + 512], in0=etmp, scalar=1.0,
                    in1=rtmp, op0=ALU.min, op1=ALU.add),
                    reads=[R_etmp, R_rtmp], writes=[rdst])

        def qkv_v(sl, banks, vg):
            for g in (vg,):
                for f in range(2):
                    b = banks[(2 * g + f) % 2]
                    for c in range(8):
                        S.emit("tensor", lambda e, g=g, f=f, b=b, c=c, sl=sl: e.matmul(
                            pb[b][:, :],
                            lhsT=xf[sl][:, 512 * c + 128 * g:512 * c + 128 * g + 128],
                            rhs=wa_sb[:, 3072 * c + 2048 + 512 * f:3072 * c + 2048 + 512 * f + 512],
                            start=(c == 0), stop=(c == 7)),
                            chain=(c != 0), reads=[R_xf[sl][c], R_wa], writes=[R_pb[b]])
                    dstap = vsb[sl][:, 1056 * g + 528 * f:1056 * g + 528 * f + 528] \
                        .rearrange("p (h c) -> p h c", c=66)[:, :, 0:64]
                    S.emit("vector", lambda e, b=b, dstap=dstap: e.tensor_copy(
                        out=dstap, in_=pb[b][:, :].rearrange("p (h c) -> p h c", c=64)),
                        reads=[R_pb[b]], writes=[R_v[sl][g]])

        def scan_p1(sl, t, c):
            for j in range(8):
                b = j // 4
                S.emit("tensor", lambda e, j=j, c=c, b=b, sl=sl: e.matmul(
                    pb[b][:, 128 * (j % 4):128 * (j % 4) + 128],
                    lhsT=ksb[sl][:, 512 * j + 128 * c:512 * j + 128 * c + 128],
                    rhs=ident[:], start=(j % 4 == 0), stop=(j % 4 == 3), skip_group_check=True),
                    chain=(j % 4 != 0), reads=[R_k[sl][j], R_id], writes=[R_pb[b]])
            S.emit("vector", lambda e: e.tensor_copy(
                out=ktmc[:, 0:512], in_=pb[0][:, :]),
                reads=[R_pb[0]], writes=[R_ktmc[0]])
            S.emit("scalar", lambda e: e.activation(
                out=ktmc[:, 512:1024], in_=pb[1][:, :], func=AF.Identity),
                reads=[R_pb[1]], writes=[R_ktmc[1]])
            mb_ = mask4[:, :].unsqueeze(1).broadcast_to([128, 4, 128])
            for a2 in range(2):
                bs = 2 - 2 * a2
                for hh in (0, 2, 4, 6, 1, 3, 5, 7):
                    h = 8 * a2 + hh
                    b = bs + hh % 2
                    po = 64 * (h % 2)
                    j = h // 2
                    S.emit("tensor", lambda e, h=h, b=b, po=po, j=j, c=c, hh=hh, sl=sl: e.matmul(
                        pb[b][:, 128 * (hh // 2):128 * (hh // 2) + 128],
                        lhsT=ksb[sl][po:po + 64, 512 * j + 128 * c:512 * j + 128 * c + 128],
                        rhs=qsb[sl][po:po + 64, 512 * j + 128 * c:512 * j + 128 * c + 128],
                        start=(hh in (0, 1)), stop=(hh in (6, 7)), skip_group_check=True),
                        chain=(hh not in (0, 1)), reads=[R_k[sl][j], R_q[sl][j]], writes=[R_pb[b]])
                for bb in range(2):
                    blk = 2 * a2 + bb
                    S.emit("vector", lambda e, bb=bb, blk=blk, bs=bs, mb_=mb_: e.tensor_tensor(
                        out=atsb[:, 512 * blk:512 * blk + 512].rearrange("p (h c) -> p h c", c=128),
                        in0=pb[bs + bb][:, :].rearrange("p (h c) -> p h c", c=128),
                        in1=mb_, op=ALU.mult),
                        reads=[R_pb[bs + bb], R_mask], writes=[R_at[blk]])

        def scan_p2(sl, t, c):
            gc = (t % 4) * 4 + c
            first = gc == 0
            for a2 in range(2):
                bs = 2 - 2 * a2
                for hh in range(8):
                    h = 8 * a2 + hh
                    b = bs + hh // 4
                    col = 65 * (hh % 4)
                    po = 64 * (h % 2)
                    j = h // 2
                    if not first:
                        S.emit("tensor", lambda e, h=h, b=b, col=col, po=po, j=j, c=c, hh=hh, sl=sl: e.matmul(
                            pb[b][:, col:col + 65],
                            lhsT=qsb[sl][po:po + 64, 512 * j + 128 * c:512 * j + 128 * c + 128],
                            rhs=Ssb[po:po + 64, 65 * (h // 2):65 * (h // 2) + 65],
                            start=(hh % 4 == 0), stop=False, skip_group_check=True),
                            chain=(hh % 4 != 0), reads=[R_q[sl][j], R_Ssb[h // 8]], writes=[R_pb[b]])
                    ao = 1024 * (h // 8) + 512 * (h % 2) + 128 * ((h % 8) // 2)
                    S.emit("tensor", lambda e, h=h, b=b, col=col, ao=ao, c=c, hh=hh, first=first, sl=sl: e.matmul(
                        pb[b][:, col:col + 65],
                        lhsT=atsb[:, ao:ao + 128],
                        rhs=vsb[sl][:, 1056 * c + 66 * h:1056 * c + 66 * h + 65],
                        start=(first and hh % 4 == 0), stop=True, skip_group_check=True),
                        chain=(not (first and hh % 4 == 0)), reads=[R_at[2 * (h // 8) + h % 2], R_v[sl][c]], writes=[R_pb[b]])
                for bb in range(2):
                    base = 8 * a2 + 4 * bb
                    src = pb[bs + bb][:, 0:260].rearrange("p (h c) -> p h c", c=65)[:, :, 64]
                    S.emit("vector", lambda e, base=base, src=src, bs=bs, bb=bb: e.tensor_scalar(
                        out=dgat[:, base:base + 4], in0=src, scalar1=EPS_AT, scalar2=None, op0=ALU.add),
                        reads=[R_pb[bs + bb]], writes=[R_dgat])
                S.emit("vector", lambda e, a2=a2: e.reciprocal(
                    out=rcp[:, 8 * a2:8 * a2 + 8], in_=dgat[:, 8 * a2:8 * a2 + 8]),
                    reads=[R_dgat], writes=[R_rcp])
                for bb in range(2):
                    base = 8 * a2 + 4 * bb
                    src = pb[bs + bb][:, 0:260].rearrange("p (h c) -> p h c", c=65)[:, :, 0:64]
                    rb = rcp[:, base:base + 4].unsqueeze(2).broadcast_to([128, 4, 64])
                    dst = ytm[:, 64 * base:64 * base + 256].rearrange("p (h c) -> p h c", c=64)
                    S.emit("vector", lambda e, src=src, rb=rb, dst=dst, bs=bs, bb=bb: e.tensor_tensor(
                        out=dst, in0=src, in1=rb, op=ALU.mult),
                        reads=[R_pb[bs + bb], R_rcp], writes=[R_ytm[a2]])

        def scan_p3(sl, t, c):
            gc = (t % 4) * 4 + c
            first = gc == 0
            if first:
                for b in (5, 6):
                    S.emit("tensor", lambda e, b=b: e.matmul(
                        pb[b][:, 0:260], lhsT=ident[:], rhs=zsb,
                        start=True, stop=False, skip_group_check=True),
                        reads=[R_id, R_zsb], writes=[R_pb[b]])
            for h in range(16):
                p2 = h // 2
                b = 5 + p2 // 4
                col = 65 * (p2 % 4)
                po = 64 * (h % 2)
                S.emit("tensor", lambda e, h=h, b=b, col=col, po=po, c=c, sl=sl: e.matmul(
                    pb[b][po:po + 64, col:col + 65],
                    lhsT=ktmc[:, 64 * h:64 * h + 64],
                    rhs=vsb[sl][:, 1056 * c + 66 * h:1056 * c + 66 * h + 65],
                    start=False, stop=True, skip_group_check=True,
                    tile_position=(0, po)),
                    chain=True, reads=[R_ktmc[h // 8], R_v[sl][c]], writes=[R_pb[b]])
            S.emit("vector", lambda e: e.tensor_copy(out=Ssb[:, 0:260], in_=pb[5][:, 0:260]),
                   reads=[R_pb[5]], writes=[R_Ssb[0]])
            S.emit("scalar", lambda e: e.activation(
                out=Ssb[:, 260:520], in_=pb[6][:, 0:260], func=AF.Identity),
                reads=[R_pb[6]], writes=[R_Ssb[1]])
            for j in range(8):
                b = 2 + j // 4
                S.emit("tensor", lambda e, j=j, b=b: e.matmul(
                    pb[b][:, 128 * (j % 4):128 * (j % 4) + 128],
                    lhsT=ytm[:, 128 * j:128 * j + 128],
                    rhs=ident[:], start=(j % 4 == 0), stop=(j % 4 == 3), skip_group_check=True),
                    chain=(j % 4 != 0), reads=[R_ytm[j // 4], R_id], writes=[R_pb[b]])
            S.emit("vector", lambda e: e.tensor_copy(
                out=yfc[:, 0:512], in_=pb[2][:, :]),
                reads=[R_pb[2]], writes=[R_yfc[0]])
            S.emit("scalar", lambda e: e.activation(
                out=yfc[:, 512:1024], in_=pb[3][:, :], func=AF.Identity),
                reads=[R_pb[3]], writes=[R_yfc[1]])
            for f in range(2):
                b = f
                for j in range(8):
                    S.emit("tensor", lambda e, f=f, b=b, j=j: e.matmul(
                        pb[b][:, :],
                        lhsT=yfc[:, 128 * j:128 * j + 128],
                        rhs=wp_sb[:, 1024 * j + 512 * f:1024 * j + 512 * f + 512],
                        start=(j == 0), stop=(j == 7)),
                        chain=(j != 0), reads=[R_yfc[j // 4], R_wp], writes=[R_pb[b]])
                S.emit("vector", lambda e, f=f, b=b, c=c, sl=sl: e.scalar_tensor_tensor(
                    out=X[sl][:, 1024 * c + 512 * f:1024 * c + 512 * f + 512],
                    in0=pb[b][:, :], scalar=1.0,
                    in1=X[sl][:, 1024 * c + 512 * f:1024 * c + 512 * f + 512],
                    op0=ALU.mult, op1=ALU.add,
                    accum_out=xsums[:, 8 * sl + 2 * c + f:8 * sl + 2 * c + f + 1]),
                    reads=[R_pb[b], R_X[sl][c]], writes=[R_X[sl][c], R_xsums[sl]])

        FRONT_BANKS = (4, 7)
        BACK_BANKS23 = (2, 3)

        def transpose_xh_half(sl, banks, hlf):
            for g in (2 * hlf, 2 * hlf + 1):
                for j in range(8):
                    b = banks[j // 4]
                    S.emit("tensor", lambda e, j=j, g=g, b=b, sl=sl: e.matmul(
                        pb[b][:, 128 * (j % 4):128 * (j % 4) + 128],
                        lhsT=xh[sl][:, 1024 * g + 128 * j:1024 * g + 128 * j + 128],
                        rhs=ident[:], start=(j % 4 == 0), stop=(j % 4 == 3), skip_group_check=True),
                        chain=(j % 4 != 0), reads=[R_xh[sl][g], R_id], writes=[R_pb[b]])
                for bb in range(2):
                    dst = xf[sl][:, 2048 * bb:2048 * bb + 2048].rearrange(
                        "p (j t) -> p j t", t=512)[:, :, 128 * g:128 * g + 128]
                    S.emit("scalar", lambda e, g=g, bb=bb, sl=sl, dst=dst: e.activation(
                        out=dst, in_=pb[banks[bb]][:, :].rearrange("p (j t) -> p j t", t=128),
                        func=AF.Identity),
                        reads=[R_pb[banks[bb]]],
                        writes=[R_xf[sl][4 * bb], R_xf[sl][4 * bb + 1], R_xf[sl][4 * bb + 2], R_xf[sl][4 * bb + 3]])

        def front_sections(t):
            sl = t % 2
            seq, tl = t // 4, t % 4
            secs = []
            src = x_d[seq, 512 * tl:512 * tl + 512, :].rearrange("(g p) c -> p g c", p=128)
            secs.append(lambda src=src, sl=sl: S.emit("sync", lambda e, src=src, sl=sl: e.dma_start(
                out=X[sl][:, :].rearrange("p (g c) -> p g c", c=1024), in_=src),
                writes=list(R_X[sl]), dma=True))
            secs.append(lambda sl=sl: ln_pass(sl, FRONT_BANKS))
            secs.append(lambda sl=sl: transpose_xh_half(sl, FRONT_BANKS, 0))
            secs.append(lambda sl=sl: transpose_xh_half(sl, FRONT_BANKS, 1))
            secs.append(lambda sl=sl: ffn(sl, FRONT_BANKS, part=0))
            secs.append(lambda sl=sl: ffn(sl, FRONT_BANKS, part=1))
            secs.append(lambda sl=sl: ln_pass(sl, FRONT_BANKS, folded=True))
            secs.append(lambda sl=sl: transpose_xh_half(sl, FRONT_BANKS, 0))
            secs.append(lambda sl=sl: transpose_xh_half(sl, FRONT_BANKS, 1))
            for grp in range(8):
                secs.append(lambda sl=sl, grp=grp: qkv_qk2(sl, FRONT_BANKS, grp))
            for vg in range(4):
                secs.append(lambda sl=sl, vg=vg: qkv_v(sl, FRONT_BANKS, vg))
            secs.append(None)
            return secs

        from collections import deque
        fq = deque()

        def pump(n=1):
            for _ in range(n):
                if fq:
                    s = fq.popleft()
                    if s is not None:
                        s()

        def back(t):
            sl = t % 2
            seq, tl = t // 4, t % 4
            for c in range(4):
                scan_p1(sl, t, c)
                pump()
                scan_p2(sl, t, c)
                pump()
                scan_p3(sl, t, c)
                pump()
            ln_pass(sl, BACK_BANKS23, folded=True)
            pump()
            transpose_xh_half(sl, BACK_BANKS23, 0)
            pump()
            transpose_xh_half(sl, BACK_BANKS23, 1)
            pump()
            ffn(sl, BACK_BANKS23, pump_fn=pump)
            pump()
            dst = out_d[seq, 512 * tl:512 * tl + 512, :].rearrange("(g p) c -> p g c", p=128)
            S.emit("gpsimd", lambda e, dst=dst, sl=sl: e.dma_start(
                out=dst, in_=X[sl][:, :].rearrange("p (g c) -> p g c", c=1024)),
                reads=list(R_X[sl]), dma=True, force_inc=True)

        fq.extend(front_sections(0))
        while fq:
            s = fq.popleft()
            if s is not None:
                s()
        for t in range(NTILE):
            if t + 1 < NTILE:
                fq.extend(front_sections(t + 1))
            back(t)
            while fq:
                s = fq.popleft()
                if s is not None:
                    s()

        run_engine = S.replay(sems)

        @block.sync
        def _(sync):
            run_engine(sync, "sync")

        @block.tensor
        def _(tensor):
            run_engine(tensor, "tensor")

        @block.vector
        def _(vector):
            run_engine(vector, "vector")

        @block.scalar
        def _(scalar):
            run_engine(scalar, "scalar")

        @block.gpsimd
        def _(gpsimd):
            run_engine(gpsimd, "gpsimd")
            gpsimd.wait_ge(sems["gpsimd"], S.final_cnt["gpsimd"])

    return nc


_NC = None


def _get_nc():
    global _NC
    if _NC is None:
        _NC = build_program()
    return _NC


def kernel(L0_attn_x, L0_attn_w_attn, L0_attn_w_proj, L0_ffn_Win, L0_ffn_bin_,
           L0_ffn_W1, L0_ffn_b1, L0_ffn_W2, L0_ffn_b2, L0_ffn_Wout, L0_ffn_bout):
    x = np.ascontiguousarray(np.asarray(L0_attn_x, dtype=np.float32))
    wa = np.ascontiguousarray(np.asarray(L0_attn_w_attn, dtype=np.float32))
    wp = np.ascontiguousarray(np.asarray(L0_attn_w_proj, dtype=np.float32))
    win = np.asarray(L0_ffn_Win, dtype=np.float32)
    binv = np.asarray(L0_ffn_bin_, dtype=np.float32)
    W1 = np.asarray(L0_ffn_W1, dtype=np.float32)
    b1 = np.asarray(L0_ffn_b1, dtype=np.float32)
    W2 = np.asarray(L0_ffn_W2, dtype=np.float32)
    b2 = np.asarray(L0_ffn_b2, dtype=np.float32)
    wout = np.ascontiguousarray(np.asarray(L0_ffn_Wout, dtype=np.float32))
    bout = np.asarray(L0_ffn_bout, dtype=np.float32)

    win_r = np.ascontiguousarray(
        np.concatenate([win.reshape(8, 128, 64)] * 2, axis=2).transpose(1, 0, 2).reshape(128, 1024))
    bd1 = np.zeros((64, 1024), np.float32)
    m_idx = np.arange(64)
    for i in range(16):
        bd1[m_idx, 16 * m_idx + i] = W1[:, i]
    bd2 = np.zeros((1024, 64), np.float32)
    for i in range(16):
        bd2[16 * m_idx + i, m_idx] = W2[:, i]
    bd2_r = np.ascontiguousarray(
        np.concatenate([bd2.reshape(8, 128, 64)] * 2, axis=2).transpose(1, 0, 2).reshape(128, 1024))
    b1f = np.ascontiguousarray(b1.reshape(1024).reshape(8, 128).T)
    binb = np.ascontiguousarray(binv.reshape(64, 1))
    b2b = np.ascontiguousarray(b2.reshape(64, 1))
    bout_r = np.ascontiguousarray(bout.reshape(1, 1024))
    ident = np.eye(128, dtype=ml_dtypes.bfloat16)
    mask4 = np.ascontiguousarray(np.triu(np.ones((128, 128), np.float32)).astype(ml_dtypes.bfloat16))

    shared = dict(w_attn=wa, w_proj=wp, win_r=win_r, bd1=bd1, bd2_r=bd2_r,
                  wout=wout, bout_r=bout_r, binb=binb, b1f=b1f, b2b=b2b,
                  ident=ident, mask4=mask4)
    in_maps = []
    for i in range(NCORE):
        m = dict(shared)
        m["x"] = np.ascontiguousarray(x[2 * i:2 * i + 2])
        in_maps.append(m)

    nc = _get_nc()
    res = run_bass_kernel_spmd(nc, in_maps, core_ids=list(range(NCORE)))
    out = np.concatenate([np.asarray(res.results[i]["out"]) for i in range(NCORE)], axis=0)
    return out


# version 55
# speedup vs baseline: 1.0309x; 1.0012x over previous
import numpy as np
import ml_dtypes
import concourse.bass as bass
import concourse.mybir as mybir
from concourse.bass_utils import run_bass_kernel_spmd

F32 = mybir.dt.float32
BF16 = mybir.dt.bfloat16
AF = mybir.ActivationFunctionType
ALU = mybir.AluOpType

B, T, C = 16, 2048, 1024
NCORE = 8
H, D = 16, 64
TILE = 512
NTILE = 8
NG = 4
EPS_LN = 1e-5
EPS_AT = 1e-6
INV_C = 1.0 / 1024.0

ENGS = ("sync", "tensor", "vector", "scalar", "gpsimd")


class Res:
    __slots__ = ("w", "r")

    def __init__(self):
        self.w = []
        self.r = []


class Sched:

    def __init__(self):
        self.ops = []
        self.forced = set()
        self.chained = set()

    def emit(self, eng, fn, reads=(), writes=(), deps=(), dma=False, force_inc=False, chain=False):
        deps = set(deps)
        for r in reads:
            deps.update(r.w)
        for r in writes:
            deps.update(r.w)
            deps.update(r.r)
        i = len(self.ops)
        self.ops.append((eng, fn, sorted(deps), 16 if dma else 1))
        if chain:
            self.chained.add(i)
        if force_inc or dma:
            self.forced.add(i)
        for r in reads:
            r.r.append(i)
        for r in writes:
            r.w = [i]
            r.r = []
        return i

    def replay(self, sems):
        n = len(self.ops)
        need = [False] * n
        for i in self.forced:
            need[i] = True
        self_sync = {}
        prev_own = {}
        for i, (eng, fn, deps, amt) in enumerate(self.ops):
            p = prev_own.get(eng)
            if p is not None and p in deps and i not in self.chained:
                self_sync[i] = p
                need[p] = True
            prev_own[eng] = i
        for eng, fn, deps, amt in self.ops:
            for d in deps:
                if self.ops[d][0] != eng:
                    need[d] = True
        val = [0] * n
        cnt = {e: 0 for e in ENGS}
        for i, (eng, fn, deps, amt) in enumerate(self.ops):
            if need[i]:
                cnt[eng] += amt
                val[i] = cnt[eng]
        self.final_cnt = cnt

        def run_engine(engine_obj, ename):
            waited = {e: 0 for e in ENGS}
            own_dma = 0
            for i, (eng, fn, deps, amt) in enumerate(self.ops):
                if eng != ename:
                    continue
                if amt == 16:
                    if own_dma > 0 and waited[ename] < 16 * own_dma:
                        engine_obj.wait_ge(sems[ename], 16 * own_dma)
                        waited[ename] = 16 * own_dma
                    own_dma += 1
                want = {}
                for d in deps:
                    de = self.ops[d][0]
                    if de == ename:
                        continue
                    v = val[d]
                    assert v > 0, f"dep {d} of op {i} has no inc"
                    if v > want.get(de, 0):
                        want[de] = v
                if i in self_sync:
                    v = val[self_sync[i]]
                    if v > want.get(ename, 0):
                        want[ename] = v
                for de, v in want.items():
                    if waited[de] < v:
                        engine_obj.wait_ge(sems[de], v)
                        waited[de] = v
                ins = fn(engine_obj)
                if need[i]:
                    ins.then_inc(sems[eng], amt)
        return run_engine


def build_program():
    nc = bass.Bass("TRN2", target_bir_lowering=False)

    x_d = nc.dram_tensor("x", [2, T, C], F32, kind="ExternalInput").ap()
    out_d = nc.dram_tensor("out", [2, T, C], F32, kind="ExternalOutput").ap()
    wa_d = nc.dram_tensor("w_attn", [C, 3 * C], F32, kind="ExternalInput").ap()
    wp_d = nc.dram_tensor("w_proj", [C, C], F32, kind="ExternalInput").ap()
    win_d = nc.dram_tensor("win_r", [128, 1024], F32, kind="ExternalInput").ap()
    bd1_d = nc.dram_tensor("bd1", [64, 1024], F32, kind="ExternalInput").ap()
    bd2_d = nc.dram_tensor("bd2_r", [128, 1024], F32, kind="ExternalInput").ap()
    wout_d = nc.dram_tensor("wout", [64, 1024], F32, kind="ExternalInput").ap()
    bout_d = nc.dram_tensor("bout_r", [1, 1024], F32, kind="ExternalInput").ap()
    binb_d = nc.dram_tensor("binb", [64, 1], F32, kind="ExternalInput").ap()
    b1f_d = nc.dram_tensor("b1f", [128, 8], F32, kind="ExternalInput").ap()
    b2b_d = nc.dram_tensor("b2b", [64, 1], F32, kind="ExternalInput").ap()
    id_d = nc.dram_tensor("ident", [128, 128], BF16, kind="ExternalInput").ap()
    mask_d = nc.dram_tensor("mask4", [128, 128], BF16, kind="ExternalInput").ap()

    from contextlib import ExitStack
    with ExitStack() as _ctx:
        def sb(name, shape, dt):
            return _ctx.enter_context(nc.sbuf_tensor(name, shape, dt))
        wa_sb = sb("wa_sb", [128, 8 * 3072], BF16)
        wp_sb = sb("wp_sb", [128, 8 * 1024], BF16)
        win_sb = sb("win_sb", [128, 1024], BF16)
        bd1_sb = sb("bd1_sb", [64, 1024], BF16)
        bd2_sb = sb("bd2_sb", [128, 1024], BF16)
        woa_sb = sb("woa_sb", [65, 1024], BF16)
        ident = sb("ident_sb", [128, 128], BF16)
        mask4 = sb("mask4_sb", [128, 128], BF16)
        binb = sb("binb_sb", [64, 1], F32)
        b1f = sb("b1f_sb", [128, 8], F32)
        b2b = sb("b2b_sb", [64, 1], F32)
        stgA = sb("stgA", [128, 1536], F32)
        stgB = sb("stgB", [128, 1536], F32)
        X = [sb("X0", [128, 4096], F32), sb("X1", [128, 4096], F32)]
        xh = [sb("xh0", [128, 4096], BF16), sb("xh1", [128, 4096], BF16)]
        xf = [sb("xf0", [128, 4096], BF16), sb("xf1", [128, 4096], BF16)]
        qsb = [sb("qsb0", [128, 4096], BF16), sb("qsb1", [128, 4096], BF16)]
        ksb = [sb("ksb0", [128, 4096], BF16), sb("ksb1", [128, 4096], BF16)]
        vsb = [sb("vsb0", [128, 4224], BF16), sb("vsb1", [128, 4224], BF16)]
        ssbf = [sb("ssbf0", [64, 512], BF16), sb("ssbf1", [64, 512], BF16)]
        gsb = [sb("gsb0", [65, 512], BF16), sb("gsb1", [65, 512], BF16)]
        stats = sb("stats", [128, 64], F32)
        xsums = sb("xsums", [128, 16], F32)
        epsb = sb("epsb", [128, 1], F32)
        atsb = sb("atsb", [128, 2048], BF16)
        Ssb = sb("Ssb", [128, 520], BF16)
        dgat = sb("dgat", [128, 16], F32)
        rcp = sb("rcp", [128, 16], F32)
        stgA_bf = stgA[:, :].bitcast(BF16)
        stgB_bf = stgB[:, :].bitcast(BF16)
        etmp = stgA_bf[:, 0:512]
        rtmp = stgA_bf[:, 512:1024]
        yfc = stgA_bf[:, 1024:2048]
        ktmc = stgB_bf[:, 0:1024]
        zsb = stgB_bf[:, 1024:1284]
        ytm = stgB_bf[:, 1284:2308]
        pb = [_ctx.enter_context(nc.psum_tensor(f"pb{i}", [128, 512], F32)) for i in range(8)]
        sems = {e: _ctx.enter_context(nc.semaphore(f"sem_{e}")) for e in ENGS}
        block = _ctx.enter_context(nc.Block())

        S = Sched()
        R_pb = [Res() for _ in range(8)]
        R_stgA, R_stgB = Res(), Res()
        R_wa, R_wp, R_win, R_bd1, R_bd2, R_woa = (Res() for _ in range(6))
        R_id, R_mask, R_binb, R_b1f, R_b2b, R_zsb = (Res() for _ in range(6))
        R_X = [[Res() for _ in range(NG)] for _ in range(2)]
        R_xh = [[Res() for _ in range(NG)] for _ in range(2)]
        R_xf = [[Res() for _ in range(8)] for _ in range(2)]
        R_q = [[Res() for _ in range(8)] for _ in range(2)]
        R_k = [[Res() for _ in range(8)] for _ in range(2)]
        R_v = [[Res() for _ in range(NG)] for _ in range(2)]
        R_ssbf = [Res(), Res()]
        R_gsb = [Res(), Res()]
        R_ktmc = [Res(), Res()]
        R_at = [Res() for _ in range(4)]
        R_Ssb = [Res(), Res()]
        R_ytm = [Res(), Res()]
        R_yfc = [Res(), Res()]
        R_etmp, R_rtmp = Res(), Res()
        R_dumpv, R_dumps = Res(), Res()
        R_sums = [Res(), Res()]
        R_sumsq = [Res(), Res()]
        R_mu = [Res(), Res()]
        R_musq = [Res(), Res()]
        R_varr = [Res(), Res()]
        R_stdv = [Res(), Res()]
        R_rr = [Res(), Res()]
        R_nmur = [Res(), Res()]
        R_dgat, R_rcp = Res(), Res()
        R_xsums = [Res(), Res()]
        R_epsb = Res()

        S.emit("sync", lambda e: e.dma_start(out=ident[:], in_=id_d), writes=[R_id], dma=True)
        S.emit("sync", lambda e: e.dma_start(out=mask4[:], in_=mask_d), writes=[R_mask], dma=True)
        S.emit("sync", lambda e: e.dma_start(out=binb[:], in_=binb_d), writes=[R_binb], dma=True)
        S.emit("sync", lambda e: e.dma_start(out=b1f[:], in_=b1f_d), writes=[R_b1f], dma=True)
        S.emit("sync", lambda e: e.dma_start(out=b2b[:], in_=b2b_d), writes=[R_b2b], dma=True)

        S.emit("vector", lambda e: e.memset(epsb[:, :], EPS_LN), writes=[R_epsb])
        for sl in range(2):
            S.emit("vector", lambda e, sl=sl: e.memset(gsb[sl][64:65, :], 1.0), writes=[R_gsb[sl]])
            for g in range(NG):
                ap = vsb[sl][:, 1056 * g:1056 * g + 1056].rearrange("p (h c) -> p h c", c=66)[:, :, 64]
                S.emit("vector", lambda e, ap=ap: e.memset(ap, 1.0), writes=[R_v[sl][g]])

        _stg_flip = [0]

        def staged(src_ap, rows, cols, dst_ap, wres, scale=None):
            stg, rstg = (stgA, R_stgA) if _stg_flip[0] == 0 else (stgB, R_stgB)
            q = "gpsimd" if _stg_flip[0] == 0 else "sync"
            _stg_flip[0] ^= 1
            sl = stg[0:rows, 0:cols]
            S.emit(q, lambda e, s=src_ap, d=sl: e.dma_start(out=d, in_=s),
                   writes=[rstg], dma=True)
            if scale is None:
                S.emit("vector", lambda e, d=sl, o=dst_ap: e.tensor_copy(out=o, in_=d),
                       reads=[rstg], writes=[wres])
            else:
                S.emit("vector", lambda e, d=sl, o=dst_ap: e.tensor_scalar(
                    out=o, in0=d, scalar1=scale, scalar2=None, op0=ALU.mult),
                    reads=[rstg], writes=[wres])

        staged(win_d, 128, 1024, win_sb[:], R_win)
        staged(bd1_d, 64, 1024, bd1_sb[:], R_bd1)
        staged(bd2_d, 128, 1024, bd2_sb[:], R_bd2)
        stg, rstg = (stgA, R_stgA) if _stg_flip[0] == 0 else (stgB, R_stgB)
        _stg_flip[0] ^= 1
        S.emit("gpsimd", lambda e, stg=stg: e.dma_start(out=stg[0:64, 0:1024], in_=wout_d),
               writes=[rstg], dma=True)
        S.emit("gpsimd", lambda e, stg=stg: e.dma_start(out=stg[64:65, 0:1024], in_=bout_d),
               writes=[rstg], dma=True)
        S.emit("vector", lambda e, stg=stg: e.tensor_scalar(
            out=woa_sb[0:65, :], in0=stg[0:65, 0:1024], scalar1=0.5, scalar2=None, op0=ALU.mult),
            reads=[rstg], writes=[R_woa])
        for j in range(8):
            for hlf in range(2):
                staged(wa_d[128 * j:128 * j + 128, 1536 * hlf:1536 * hlf + 1536], 128, 1536,
                       wa_sb[:, 3072 * j + 1536 * hlf:3072 * j + 1536 * hlf + 1536], R_wa)
        for j in range(8):
            staged(wp_d[128 * j:128 * j + 128, :], 128, 1024,
                   wp_sb[:, 1024 * j:1024 * j + 1024], R_wp)
        for r in (R_etmp, R_rtmp, R_yfc[0], R_yfc[1]):
            r.r.extend(R_stgA.r)
            r.w = list(R_stgA.w)
        for r in (R_ktmc[0], R_ktmc[1], R_zsb, R_ytm[0], R_ytm[1]):
            r.r.extend(R_stgB.r)
            r.w = list(R_stgB.w)
        S.emit("vector", lambda e: e.memset(zsb, 0.0), writes=[R_zsb])

        def ln_pass(sl, fb, folded=False):
            o = 32 * sl
            sums = stats[:, o + 0:o + 4]
            sumsq = stats[:, o + 4:o + 8]
            mu = stats[:, o + 8:o + 12]
            musq = stats[:, o + 12:o + 16]
            varr = stats[:, o + 16:o + 20]
            stdv = stats[:, o + 20:o + 24]
            rr = stats[:, o + 24:o + 28]
            nmur = stats[:, o + 28:o + 32]
            for g in range(NG):
                if not folded:
                    S.emit("vector", lambda e, g=g, sl=sl, sums=sums: e.tensor_scalar(
                        out=xh[sl][:, 0:1024], in0=X[sl][:, 1024 * g:1024 * g + 1024],
                        scalar1=1.0, scalar2=0.0, op0=ALU.mult, op1=ALU.add,
                        accum_out=sums[:, g:g + 1]),
                        reads=[R_X[sl][g]], writes=[R_xh[sl][0], R_sums[sl]])
                S.emit("scalar", lambda e, g=g, sl=sl, sumsq=sumsq: e.activation(
                    out=xh[sl][:, 1024:2048], in_=X[sl][:, 1024 * g:1024 * g + 1024],
                    func=AF.Square, accum_out=sumsq[:, g:g + 1]),
                    reads=[R_X[sl][g]], writes=[R_xh[sl][1], R_sumsq[sl]])
            if folded:
                S.emit("vector", lambda e, sl=sl, sums=sums: e.tensor_tensor(
                    out=sums,
                    in0=xsums[:, 8 * sl:8 * sl + 8].rearrange("p (g f) -> p g f", f=2)[:, :, 0],
                    in1=xsums[:, 8 * sl:8 * sl + 8].rearrange("p (g f) -> p g f", f=2)[:, :, 1],
                    op=ALU.add),
                    reads=[R_xsums[sl]], writes=[R_sums[sl]])
            S.emit("vector", lambda e, sums=sums, mu=mu: e.tensor_scalar(
                out=mu, in0=sums, scalar1=INV_C, scalar2=None, op0=ALU.mult),
                reads=[R_sums[sl]], writes=[R_mu[sl]])
            S.emit("vector", lambda e, mu=mu, musq=musq: e.scalar_tensor_tensor(
                out=musq, in0=mu, scalar=1.0, in1=mu, op0=ALU.mult, op1=ALU.mult),
                reads=[R_mu[sl]], writes=[R_musq[sl]])
            S.emit("vector", lambda e, sumsq=sumsq, musq=musq, varr=varr: e.scalar_tensor_tensor(
                out=varr, in0=sumsq, scalar=INV_C, in1=musq, op0=ALU.mult, op1=ALU.subtract),
                reads=[R_sumsq[sl], R_musq[sl]], writes=[R_varr[sl]])
            S.emit("scalar", lambda e, varr=varr, stdv=stdv: e.activation(out=stdv, in_=varr, func=AF.Ln,
                                                                          bias=epsb[:, :]),
                   reads=[R_varr[sl], R_epsb], writes=[R_stdv[sl]])
            S.emit("scalar", lambda e, stdv=stdv, rr=rr: e.activation(out=rr, in_=stdv, func=AF.Exp, scale=-0.5),
                   reads=[R_stdv[sl]], writes=[R_rr[sl]])
            S.emit("vector", lambda e, mu=mu, rr=rr, nmur=nmur: e.scalar_tensor_tensor(
                out=nmur, in0=mu, scalar=-1.0, in1=rr, op0=ALU.mult, op1=ALU.mult),
                reads=[R_mu[sl], R_rr[sl]], writes=[R_nmur[sl]])
            for g in range(NG):
                if g % 2 == 0:
                    S.emit("scalar", lambda e, g=g, sl=sl, rr=rr, nmur=nmur: e.activation(
                        out=xh[sl][:, 1024 * g:1024 * g + 1024], in_=X[sl][:, 1024 * g:1024 * g + 1024],
                        func=AF.Identity, scale=rr[:, g:g + 1], bias=nmur[:, g:g + 1]),
                        reads=[R_X[sl][g], R_rr[sl], R_nmur[sl]], writes=[R_xh[sl][g]])
                else:
                    S.emit("vector", lambda e, g=g, sl=sl, rr=rr, nmur=nmur: e.tensor_scalar(
                        out=xh[sl][:, 1024 * g:1024 * g + 1024], in0=X[sl][:, 1024 * g:1024 * g + 1024],
                        scalar1=rr[:, g:g + 1], scalar2=nmur[:, g:g + 1], op0=ALU.mult, op1=ALU.add),
                        reads=[R_X[sl][g], R_rr[sl], R_nmur[sl]], writes=[R_xh[sl][g]])

        def transpose_xh(sl, banks):
            for j in range(8):
                b = banks[j % 2]
                for g in range(NG):
                    S.emit("tensor", lambda e, j=j, g=g, b=b, sl=sl: e.matmul(
                        pb[b][:, 128 * g:128 * g + 128],
                        lhsT=xh[sl][:, 1024 * g + 128 * j:1024 * g + 128 * j + 128],
                        rhs=ident[:], start=(g == 0), stop=(g == 3), skip_group_check=True),
                        chain=(g != 0), reads=[R_xh[sl][g], R_id], writes=[R_pb[b]])
                if j % 2 == 0:
                    S.emit("vector", lambda e, j=j, b=b, sl=sl: e.tensor_copy(
                        out=xf[sl][:, 512 * j:512 * j + 512], in_=pb[b][:, :]),
                        reads=[R_pb[b]], writes=[R_xf[sl][j]])
                else:
                    S.emit("scalar", lambda e, j=j, b=b, sl=sl: e.activation(
                        out=xf[sl][:, 512 * j:512 * j + 512], in_=pb[b][:, :], func=AF.Identity),
                        reads=[R_pb[b]], writes=[R_xf[sl][j]])

        def ffn(sl, banks, pump_fn=None, part=None):
            b0, b1 = banks
            hsb, R_h = qsb[sl], R_q[sl]
            if part == 1:
                _ffn_part2(sl, banks, hsb, R_h, b0, pump_fn)
                return
            for j in range(8):
                S.emit("tensor", lambda e, j=j, sl=sl, b0=b0: e.matmul(
                    pb[b0][:, :], lhsT=win_sb[:, 128 * j:128 * j + 128],
                    rhs=xf[sl][:, 512 * j:512 * j + 512], start=(j == 0), stop=(j == 7)),
                    chain=(j != 0), reads=[R_xf[sl][j], R_win], writes=[R_pb[b0]])
            S.emit("scalar", lambda e, sl=sl, b0=b0: e.activation(
                out=ssbf[sl][:, :], in_=pb[b0][0:64, :], func=AF.Identity, bias=binb[:, :]),
                reads=[R_pb[b0], R_binb], writes=[R_ssbf[sl]])
            if pump_fn:
                pump_fn()
            for j in range(8):
                b = banks[j % 2]
                S.emit("tensor", lambda e, j=j, b=b, sl=sl: e.matmul(
                    pb[b][:, :], lhsT=bd1_sb[:, 128 * j:128 * j + 128],
                    rhs=ssbf[sl][:, :], start=True, stop=True),
                    reads=[R_ssbf[sl], R_bd1], writes=[R_pb[b]])
                if j % 2 == 0:
                    S.emit("scalar", lambda e, j=j, b=b, hsb=hsb: e.activation(
                        out=hsb[:, 512 * j:512 * j + 512], in_=pb[b][:, :],
                        func=AF.Relu, bias=b1f[:, j:j + 1]),
                        reads=[R_pb[b], R_b1f], writes=[R_h[j]])
                else:
                    S.emit("vector", lambda e, j=j, b=b, hsb=hsb: e.tensor_scalar(
                        out=hsb[:, 512 * j:512 * j + 512], in0=pb[b][:, :],
                        scalar1=b1f[:, j:j + 1], scalar2=0.0,
                        op0=ALU.add, op1=ALU.max),
                        reads=[R_pb[b], R_b1f], writes=[R_h[j]])
            if part == 0:
                return
            _ffn_part2(sl, banks, hsb, R_h, b0, pump_fn)

        def _ffn_part2(sl, banks, hsb, R_h, b0, pump_fn=None):
            for j in range(8):
                S.emit("tensor", lambda e, j=j, sl=sl, b0=b0, hsb=hsb: e.matmul(
                    pb[b0][:, :], lhsT=bd2_sb[:, 128 * j:128 * j + 128],
                    rhs=hsb[:, 512 * j:512 * j + 512], start=(j == 0), stop=(j == 7)),
                    chain=(j != 0), reads=[R_h[j], R_bd2], writes=[R_pb[b0]])
            S.emit("scalar", lambda e, sl=sl, b0=b0: e.activation(
                out=gsb[sl][0:64, :], in_=pb[b0][0:64, :], func=AF.Identity, bias=b2b[:, :]),
                reads=[R_pb[b0], R_b2b], writes=[R_gsb[sl]])
            if pump_fn:
                pump_fn()
            for g in range(NG):
                for f in range(2):
                    b = banks[(2 * g + f) % 2]
                    S.emit("tensor", lambda e, g=g, f=f, b=b, sl=sl: e.matmul(
                        pb[b][:, :], lhsT=gsb[sl][:, 128 * g:128 * g + 128],
                        rhs=woa_sb[:, 512 * f:512 * f + 512], start=True, stop=True),
                        reads=[R_gsb[sl], R_woa], writes=[R_pb[b]])
                    S.emit("vector", lambda e, g=g, f=f, b=b, sl=sl: e.scalar_tensor_tensor(
                        out=X[sl][:, 1024 * g + 512 * f:1024 * g + 512 * f + 512],
                        in0=pb[b][:, :], scalar=1.0,
                        in1=X[sl][:, 1024 * g + 512 * f:1024 * g + 512 * f + 512],
                        op0=ALU.mult, op1=ALU.add,
                        accum_out=xsums[:, 8 * sl + 2 * g + f:8 * sl + 2 * g + f + 1]),
                        reads=[R_pb[b], R_X[sl][g]], writes=[R_X[sl][g], R_xsums[sl]])

        def qkv_qk2(sl, banks, grp):
            for k16 in range(2 * grp, 2 * grp + 2):
                isq = k16 < 8
                j = k16 % 8
                b = banks[k16 % 2]
                dst = qsb[sl] if isq else ksb[sl]
                rdst = R_q[sl][j] if isq else R_k[sl][j]
                for c in range(8):
                    S.emit("tensor", lambda e, k16=k16, b=b, c=c, sl=sl: e.matmul(
                        pb[b][:, :],
                        lhsT=wa_sb[:, 3072 * c + 128 * k16:3072 * c + 128 * k16 + 128],
                        rhs=xf[sl][:, 512 * c:512 * c + 512], start=(c == 0), stop=(c == 7)),
                        chain=(c != 0), reads=[R_xf[sl][c], R_wa], writes=[R_pb[b]])
                S.emit("scalar", lambda e, b=b: e.activation(out=etmp, in_=pb[b][:, :], func=AF.Exp),
                       reads=[R_pb[b]], writes=[R_etmp])
                S.emit("scalar", lambda e, b=b: e.activation(out=rtmp, in_=pb[b][:, :], func=AF.Relu),
                       reads=[R_pb[b]], writes=[R_rtmp])
                S.emit("vector", lambda e, j=j, dst=dst: e.scalar_tensor_tensor(
                    out=dst[:, 512 * j:512 * j + 512], in0=etmp, scalar=1.0,
                    in1=rtmp, op0=ALU.min, op1=ALU.add),
                    reads=[R_etmp, R_rtmp], writes=[rdst])

        def qkv_v(sl, banks, vg):
            for g in (vg,):
                for f in range(2):
                    b = banks[(2 * g + f) % 2]
                    for c in range(8):
                        S.emit("tensor", lambda e, g=g, f=f, b=b, c=c, sl=sl: e.matmul(
                            pb[b][:, :],
                            lhsT=xf[sl][:, 512 * c + 128 * g:512 * c + 128 * g + 128],
                            rhs=wa_sb[:, 3072 * c + 2048 + 512 * f:3072 * c + 2048 + 512 * f + 512],
                            start=(c == 0), stop=(c == 7)),
                            chain=(c != 0), reads=[R_xf[sl][c], R_wa], writes=[R_pb[b]])
                    dstap = vsb[sl][:, 1056 * g + 528 * f:1056 * g + 528 * f + 528] \
                        .rearrange("p (h c) -> p h c", c=66)[:, :, 0:64]
                    S.emit("vector", lambda e, b=b, dstap=dstap: e.tensor_copy(
                        out=dstap, in_=pb[b][:, :].rearrange("p (h c) -> p h c", c=64)),
                        reads=[R_pb[b]], writes=[R_v[sl][g]])

        def scan_p1(sl, t, c):
            for j in range(8):
                b = j // 4
                S.emit("tensor", lambda e, j=j, c=c, b=b, sl=sl: e.matmul(
                    pb[b][:, 128 * (j % 4):128 * (j % 4) + 128],
                    lhsT=ksb[sl][:, 512 * j + 128 * c:512 * j + 128 * c + 128],
                    rhs=ident[:], start=(j % 4 == 0), stop=(j % 4 == 3), skip_group_check=True),
                    chain=(j % 4 != 0), reads=[R_k[sl][j], R_id], writes=[R_pb[b]])
            S.emit("vector", lambda e: e.tensor_copy(
                out=ktmc[:, 0:512], in_=pb[0][:, :]),
                reads=[R_pb[0]], writes=[R_ktmc[0]])
            S.emit("scalar", lambda e: e.activation(
                out=ktmc[:, 512:1024], in_=pb[1][:, :], func=AF.Identity),
                reads=[R_pb[1]], writes=[R_ktmc[1]])
            mb_ = mask4[:, :].unsqueeze(1).broadcast_to([128, 4, 128])
            for a2 in range(2):
                bs = 2 - 2 * a2
                for hh in (0, 2, 4, 6, 1, 3, 5, 7):
                    h = 8 * a2 + hh
                    b = bs + hh % 2
                    po = 64 * (h % 2)
                    j = h // 2
                    S.emit("tensor", lambda e, h=h, b=b, po=po, j=j, c=c, hh=hh, sl=sl: e.matmul(
                        pb[b][:, 128 * (hh // 2):128 * (hh // 2) + 128],
                        lhsT=ksb[sl][po:po + 64, 512 * j + 128 * c:512 * j + 128 * c + 128],
                        rhs=qsb[sl][po:po + 64, 512 * j + 128 * c:512 * j + 128 * c + 128],
                        start=(hh in (0, 1)), stop=(hh in (6, 7)), skip_group_check=True),
                        chain=(hh not in (0, 1)), reads=[R_k[sl][j], R_q[sl][j]], writes=[R_pb[b]])
                for bb in range(2):
                    blk = 2 * a2 + bb
                    S.emit("vector", lambda e, bb=bb, blk=blk, bs=bs, mb_=mb_: e.tensor_tensor(
                        out=atsb[:, 512 * blk:512 * blk + 512].rearrange("p (h c) -> p h c", c=128),
                        in0=pb[bs + bb][:, :].rearrange("p (h c) -> p h c", c=128),
                        in1=mb_, op=ALU.mult),
                        reads=[R_pb[bs + bb], R_mask], writes=[R_at[blk]])

        def scan_p2(sl, t, c):
            gc = (t % 4) * 4 + c
            first = gc == 0
            for a2 in range(2):
                bs = 2 - 2 * a2
                for hh in range(8):
                    h = 8 * a2 + hh
                    b = bs + hh // 4
                    col = 65 * (hh % 4)
                    po = 64 * (h % 2)
                    j = h // 2
                    if not first:
                        S.emit("tensor", lambda e, h=h, b=b, col=col, po=po, j=j, c=c, hh=hh, sl=sl: e.matmul(
                            pb[b][:, col:col + 65],
                            lhsT=qsb[sl][po:po + 64, 512 * j + 128 * c:512 * j + 128 * c + 128],
                            rhs=Ssb[po:po + 64, 65 * (h // 2):65 * (h // 2) + 65],
                            start=(hh % 4 == 0), stop=False, skip_group_check=True),
                            chain=(hh % 4 != 0), reads=[R_q[sl][j], R_Ssb[h // 8]], writes=[R_pb[b]])
                    ao = 1024 * (h // 8) + 512 * (h % 2) + 128 * ((h % 8) // 2)
                    S.emit("tensor", lambda e, h=h, b=b, col=col, ao=ao, c=c, hh=hh, first=first, sl=sl: e.matmul(
                        pb[b][:, col:col + 65],
                        lhsT=atsb[:, ao:ao + 128],
                        rhs=vsb[sl][:, 1056 * c + 66 * h:1056 * c + 66 * h + 65],
                        start=(first and hh % 4 == 0), stop=True, skip_group_check=True),
                        chain=(not (first and hh % 4 == 0)), reads=[R_at[2 * (h // 8) + h % 2], R_v[sl][c]], writes=[R_pb[b]])
                for bb in range(2):
                    base = 8 * a2 + 4 * bb
                    src = pb[bs + bb][:, 0:260].rearrange("p (h c) -> p h c", c=65)[:, :, 64]
                    S.emit("vector", lambda e, base=base, src=src, bs=bs, bb=bb: e.tensor_scalar(
                        out=dgat[:, base:base + 4], in0=src, scalar1=EPS_AT, scalar2=None, op0=ALU.add),
                        reads=[R_pb[bs + bb]], writes=[R_dgat])
                S.emit("vector", lambda e, a2=a2: e.reciprocal(
                    out=rcp[:, 8 * a2:8 * a2 + 8], in_=dgat[:, 8 * a2:8 * a2 + 8]),
                    reads=[R_dgat], writes=[R_rcp])
                for bb in range(2):
                    base = 8 * a2 + 4 * bb
                    src = pb[bs + bb][:, 0:260].rearrange("p (h c) -> p h c", c=65)[:, :, 0:64]
                    rb = rcp[:, base:base + 4].unsqueeze(2).broadcast_to([128, 4, 64])
                    dst = ytm[:, 64 * base:64 * base + 256].rearrange("p (h c) -> p h c", c=64)
                    S.emit("vector", lambda e, src=src, rb=rb, dst=dst, bs=bs, bb=bb: e.tensor_tensor(
                        out=dst, in0=src, in1=rb, op=ALU.mult),
                        reads=[R_pb[bs + bb], R_rcp], writes=[R_ytm[a2]])

        def scan_p3(sl, t, c):
            gc = (t % 4) * 4 + c
            first = gc == 0
            if first:
                for b in (5, 6):
                    S.emit("tensor", lambda e, b=b: e.matmul(
                        pb[b][:, 0:260], lhsT=ident[:], rhs=zsb,
                        start=True, stop=False, skip_group_check=True),
                        reads=[R_id, R_zsb], writes=[R_pb[b]])
            for h in range(16):
                p2 = h // 2
                b = 5 + p2 // 4
                col = 65 * (p2 % 4)
                po = 64 * (h % 2)
                S.emit("tensor", lambda e, h=h, b=b, col=col, po=po, c=c, sl=sl: e.matmul(
                    pb[b][po:po + 64, col:col + 65],
                    lhsT=ktmc[:, 64 * h:64 * h + 64],
                    rhs=vsb[sl][:, 1056 * c + 66 * h:1056 * c + 66 * h + 65],
                    start=False, stop=True, skip_group_check=True,
                    tile_position=(0, po)),
                    chain=True, reads=[R_ktmc[h // 8], R_v[sl][c]], writes=[R_pb[b]])
            S.emit("vector", lambda e: e.tensor_copy(out=Ssb[:, 0:260], in_=pb[5][:, 0:260]),
                   reads=[R_pb[5]], writes=[R_Ssb[0]])
            S.emit("scalar", lambda e: e.activation(
                out=Ssb[:, 260:520], in_=pb[6][:, 0:260], func=AF.Identity),
                reads=[R_pb[6]], writes=[R_Ssb[1]])
            for j in range(8):
                b = 2 + j // 4
                S.emit("tensor", lambda e, j=j, b=b: e.matmul(
                    pb[b][:, 128 * (j % 4):128 * (j % 4) + 128],
                    lhsT=ytm[:, 128 * j:128 * j + 128],
                    rhs=ident[:], start=(j % 4 == 0), stop=(j % 4 == 3), skip_group_check=True),
                    chain=(j % 4 != 0), reads=[R_ytm[j // 4], R_id], writes=[R_pb[b]])
            S.emit("vector", lambda e: e.tensor_copy(
                out=yfc[:, 0:512], in_=pb[2][:, :]),
                reads=[R_pb[2]], writes=[R_yfc[0]])
            S.emit("scalar", lambda e: e.activation(
                out=yfc[:, 512:1024], in_=pb[3][:, :], func=AF.Identity),
                reads=[R_pb[3]], writes=[R_yfc[1]])
            for f in range(2):
                b = f
                for j in range(8):
                    S.emit("tensor", lambda e, f=f, b=b, j=j: e.matmul(
                        pb[b][:, :],
                        lhsT=yfc[:, 128 * j:128 * j + 128],
                        rhs=wp_sb[:, 1024 * j + 512 * f:1024 * j + 512 * f + 512],
                        start=(j == 0), stop=(j == 7)),
                        chain=(j != 0), reads=[R_yfc[j // 4], R_wp], writes=[R_pb[b]])
                S.emit("vector", lambda e, f=f, b=b, c=c, sl=sl: e.scalar_tensor_tensor(
                    out=X[sl][:, 1024 * c + 512 * f:1024 * c + 512 * f + 512],
                    in0=pb[b][:, :], scalar=1.0,
                    in1=X[sl][:, 1024 * c + 512 * f:1024 * c + 512 * f + 512],
                    op0=ALU.mult, op1=ALU.add,
                    accum_out=xsums[:, 8 * sl + 2 * c + f:8 * sl + 2 * c + f + 1]),
                    reads=[R_pb[b], R_X[sl][c]], writes=[R_X[sl][c], R_xsums[sl]])

        FRONT_BANKS = (4, 7)
        BACK_BANKS23 = (2, 3)

        def transpose_xh_half(sl, banks, hlf):
            for g in (2 * hlf, 2 * hlf + 1):
                for j in range(8):
                    b = banks[j // 4]
                    S.emit("tensor", lambda e, j=j, g=g, b=b, sl=sl: e.matmul(
                        pb[b][:, 128 * (j % 4):128 * (j % 4) + 128],
                        lhsT=xh[sl][:, 1024 * g + 128 * j:1024 * g + 128 * j + 128],
                        rhs=ident[:], start=(j % 4 == 0), stop=(j % 4 == 3), skip_group_check=True),
                        chain=(j % 4 != 0), reads=[R_xh[sl][g], R_id], writes=[R_pb[b]])
                for bb in range(2):
                    dst = xf[sl][:, 2048 * bb:2048 * bb + 2048].rearrange(
                        "p (j t) -> p j t", t=512)[:, :, 128 * g:128 * g + 128]
                    S.emit("scalar", lambda e, g=g, bb=bb, sl=sl, dst=dst: e.activation(
                        out=dst, in_=pb[banks[bb]][:, :].rearrange("p (j t) -> p j t", t=128),
                        func=AF.Identity),
                        reads=[R_pb[banks[bb]]],
                        writes=[R_xf[sl][4 * bb], R_xf[sl][4 * bb + 1], R_xf[sl][4 * bb + 2], R_xf[sl][4 * bb + 3]])

        def front_sections(t):
            sl = t % 2
            seq, tl = t // 4, t % 4
            secs = []
            src = x_d[seq, 512 * tl:512 * tl + 512, :].rearrange("(g p) c -> p g c", p=128)
            secs.append(lambda src=src, sl=sl: S.emit("sync", lambda e, src=src, sl=sl: e.dma_start(
                out=X[sl][:, :].rearrange("p (g c) -> p g c", c=1024), in_=src),
                writes=list(R_X[sl]), dma=True))
            secs.append(lambda sl=sl: ln_pass(sl, FRONT_BANKS))
            secs.append(lambda sl=sl: transpose_xh_half(sl, FRONT_BANKS, 0))
            secs.append(lambda sl=sl: transpose_xh_half(sl, FRONT_BANKS, 1))
            secs.append(lambda sl=sl: ffn(sl, FRONT_BANKS, part=0))
            secs.append(lambda sl=sl: ffn(sl, FRONT_BANKS, part=1))
            secs.append(lambda sl=sl: ln_pass(sl, FRONT_BANKS, folded=True))
            secs.append(lambda sl=sl: transpose_xh_half(sl, FRONT_BANKS, 0))
            secs.append(lambda sl=sl: transpose_xh_half(sl, FRONT_BANKS, 1))
            for grp in range(8):
                secs.append(lambda sl=sl, grp=grp: qkv_qk2(sl, FRONT_BANKS, grp))
            for vg in range(4):
                secs.append(lambda sl=sl, vg=vg: qkv_v(sl, FRONT_BANKS, vg))
            secs.append(None)
            return secs

        from collections import deque
        fq = deque()

        def pump(n=1):
            for _ in range(n):
                if fq:
                    s = fq.popleft()
                    if s is not None:
                        s()

        def back(t):
            sl = t % 2
            seq, tl = t // 4, t % 4
            for c in range(4):
                scan_p1(sl, t, c)
                pump()
                scan_p2(sl, t, c)
                pump()
                scan_p3(sl, t, c)
                pump()
            ln_pass(sl, BACK_BANKS23, folded=True)
            pump()
            transpose_xh_half(sl, BACK_BANKS23, 0)
            pump()
            transpose_xh_half(sl, BACK_BANKS23, 1)
            pump()
            ffn(sl, BACK_BANKS23, pump_fn=pump)
            pump()
            dst = out_d[seq, 512 * tl:512 * tl + 512, :].rearrange("(g p) c -> p g c", p=128)
            S.emit("gpsimd", lambda e, dst=dst, sl=sl: e.dma_start(
                out=dst, in_=X[sl][:, :].rearrange("p (g c) -> p g c", c=1024)),
                reads=list(R_X[sl]), dma=True, force_inc=True)

        fq.extend(front_sections(0))
        while fq:
            s = fq.popleft()
            if s is not None:
                s()
        for t in range(NTILE):
            if t + 1 < NTILE:
                fq.extend(front_sections(t + 1))
            back(t)
            while fq:
                s = fq.popleft()
                if s is not None:
                    s()

        run_engine = S.replay(sems)

        @block.sync
        def _(sync):
            run_engine(sync, "sync")

        @block.tensor
        def _(tensor):
            run_engine(tensor, "tensor")

        @block.vector
        def _(vector):
            run_engine(vector, "vector")

        @block.scalar
        def _(scalar):
            run_engine(scalar, "scalar")

        @block.gpsimd
        def _(gpsimd):
            run_engine(gpsimd, "gpsimd")
            gpsimd.wait_ge(sems["gpsimd"], S.final_cnt["gpsimd"])

    return nc


_NC = None


def _get_nc():
    global _NC
    if _NC is None:
        _NC = build_program()
    return _NC


def kernel(L0_attn_x, L0_attn_w_attn, L0_attn_w_proj, L0_ffn_Win, L0_ffn_bin_,
           L0_ffn_W1, L0_ffn_b1, L0_ffn_W2, L0_ffn_b2, L0_ffn_Wout, L0_ffn_bout):
    x = np.ascontiguousarray(np.asarray(L0_attn_x, dtype=np.float32))
    wa = np.ascontiguousarray(np.asarray(L0_attn_w_attn, dtype=np.float32))
    wp = np.ascontiguousarray(np.asarray(L0_attn_w_proj, dtype=np.float32))
    win = np.asarray(L0_ffn_Win, dtype=np.float32)
    binv = np.asarray(L0_ffn_bin_, dtype=np.float32)
    W1 = np.asarray(L0_ffn_W1, dtype=np.float32)
    b1 = np.asarray(L0_ffn_b1, dtype=np.float32)
    W2 = np.asarray(L0_ffn_W2, dtype=np.float32)
    b2 = np.asarray(L0_ffn_b2, dtype=np.float32)
    wout = np.ascontiguousarray(np.asarray(L0_ffn_Wout, dtype=np.float32))
    bout = np.asarray(L0_ffn_bout, dtype=np.float32)

    win_r = np.ascontiguousarray(
        np.concatenate([win.reshape(8, 128, 64)] * 2, axis=2).transpose(1, 0, 2).reshape(128, 1024))
    bd1 = np.zeros((64, 1024), np.float32)
    m_idx = np.arange(64)
    for i in range(16):
        bd1[m_idx, 16 * m_idx + i] = W1[:, i]
    bd2 = np.zeros((1024, 64), np.float32)
    for i in range(16):
        bd2[16 * m_idx + i, m_idx] = W2[:, i]
    bd2_r = np.ascontiguousarray(
        np.concatenate([bd2.reshape(8, 128, 64)] * 2, axis=2).transpose(1, 0, 2).reshape(128, 1024))
    b1f = np.ascontiguousarray(b1.reshape(1024).reshape(8, 128).T)
    binb = np.ascontiguousarray(binv.reshape(64, 1))
    b2b = np.ascontiguousarray(b2.reshape(64, 1))
    bout_r = np.ascontiguousarray(bout.reshape(1, 1024))
    ident = np.eye(128, dtype=ml_dtypes.bfloat16)
    mask4 = np.ascontiguousarray(np.triu(np.ones((128, 128), np.float32)).astype(ml_dtypes.bfloat16))

    shared = dict(w_attn=wa, w_proj=wp, win_r=win_r, bd1=bd1, bd2_r=bd2_r,
                  wout=wout, bout_r=bout_r, binb=binb, b1f=b1f, b2b=b2b,
                  ident=ident, mask4=mask4)
    in_maps = []
    for i in range(NCORE):
        m = dict(shared)
        m["x"] = np.ascontiguousarray(x[2 * i:2 * i + 2])
        in_maps.append(m)

    nc = _get_nc()
    res = run_bass_kernel_spmd(nc, in_maps, core_ids=list(range(NCORE)))
    out = np.concatenate([np.asarray(res.results[i]["out"]) for i in range(NCORE)], axis=0)
    return out
